# version 04
# baseline (speedup 1.0000x reference)
import numpy as np
import ml_dtypes
import concourse.bass as bass
import concourse.mybir as mybir
from concourse.bass_utils import run_bass_kernel_spmd
from contextlib import ExitStack

BF = ml_dtypes.bfloat16
F32 = mybir.dt.float32
BF16 = mybir.dt.bfloat16
I32 = mybir.dt.int32
AF = mybir.ActivationFunctionType
ALU = mybir.AluOpType
AXX = mybir.AxisListType.X

NCORES = 8
D = 2048
TPC = 4096
SEQ = 2048
LN_EPS = 1e-5


class Sem:
    def __init__(self, h):
        self.h = h


def wait(q, sem, n):
    q.wait_ge(sem.h, n)


class Sched:
    def __init__(self, step=1):
        self.step = step
        self.count = 0
        self.at = {}

    @staticmethod
    def _key(name):
        if len(name) == 1 and isinstance(name[0], tuple):
            return name[0]
        return name

    def add(self, *name):
        self.count += self.step
        if name:
            self.at[self._key(name)] = self.count
        return self.count

    def __getitem__(self, name):
        if not isinstance(name, tuple):
            name = (name,)
        return self.at[self._key(name)]


def host_consts():
    c = {}
    c["c_id128"] = np.eye(128, dtype=BF)
    c["c_id128f"] = np.eye(128, dtype=np.float32)
    c["c_ones_col"] = np.ones((128, 1), dtype=BF)
    c["c_ones_row"] = np.ones((1, 128), dtype=BF)
    c["c_ones_rowf"] = np.ones((1, 128), dtype=np.float32)
    c["c_neg_id64"] = (-np.eye(64, dtype=np.float32)).astype(BF)
    blk = np.arange(128) // 16
    idx = np.arange(128)
    c["c_mask_bsu"] = (blk[:, None] < blk[None, :]).astype(np.float32)
    c["c_mask_dtu"] = ((blk[:, None] == blk[None, :]) &
                       (idx[:, None] <= idx[None, :])).astype(np.float32)
    c["c_mask_su"] = (idx[:, None] < idx[None, :]).astype(np.float32)
    c["c_mask_li"] = (idx[:, None] >= idx[None, :]).astype(np.float32)
    c["c_mask_ls"] = (idx[:, None] > idx[None, :]).astype(np.float32)
    c["c_mask_ui"] = (idx[:, None] <= idx[None, :]).astype(np.float32)
    tok = np.tile(1.0 / np.arange(1, 17, dtype=np.float32), 8) / 64.0
    c["c_tokidx"] = tok.reshape(128, 1).astype(np.float32)
    misc = np.zeros((128, 8), dtype=np.float32)
    misc[:, 0] = 1e-5
    misc[:, 1] = 1e-6
    c["c_misc"] = misc
    return c


CONST_SHAPES = {
    "c_id128": ([128, 128], BF16), "c_id128f": ([128, 128], F32),
    "c_ones_col": ([128, 1], BF16), "c_ones_row": ([1, 128], BF16),
    "c_ones_rowf": ([1, 128], F32),
    "c_neg_id64": ([64, 64], BF16),
    "c_mask_bsu": ([128, 128], F32), "c_mask_dtu": ([128, 128], F32),
    "c_mask_su": ([128, 128], F32), "c_mask_li": ([128, 128], F32),
    "c_mask_ls": ([128, 128], F32), "c_mask_ui": ([128, 128], F32),
    "c_tokidx": ([128, 1], F32),
    "c_misc": ([128, 8], F32),
}


def declare_consts(nc, names):
    out = {}
    for n in names:
        sh, dt = CONST_SHAPES[n]
        out[n] = nc.dram_tensor(n, sh, dt, kind="ExternalInput").ap()
    return out


def build_phase_d():
    nc = bass.Bass("TRN2", target_bir_lowering=False)
    xT = nc.dram_tensor("xT", [D, TPC], F32, kind="ExternalInput").ap()
    gwq = nc.dram_tensor("L1_ffn_gate_w", [8192, 2048], I32, kind="ExternalInput").ap()
    uwq = nc.dram_tensor("L1_ffn_up_w", [8192, 2048], I32, kind="ExternalInput").ap()
    dwq = nc.dram_tensor("L1_ffn_down_w", [2048, 8192], I32, kind="ExternalInput").ap()
    gsc = nc.dram_tensor("L1_ffn_gate_scale", [8192], F32, kind="ExternalInput").ap()
    usc = nc.dram_tensor("L1_ffn_up_scale", [8192], F32, kind="ExternalInput").ap()
    dsc = nc.dram_tensor("L1_ffn_down_scale", [2048], F32, kind="ExternalInput").ap()
    y = nc.dram_tensor("y", [2, SEQ, D], F32, kind="ExternalOutput").ap()
    C = declare_consts(nc, ["c_ones_col", "c_ones_row", "c_id128f", "c_misc"])

    gw_sc = nc.dram_tensor("gw_sc", [8192, 2048], BF16, kind="Internal").ap()
    uw_sc = nc.dram_tensor("uw_sc", [8192, 2048], BF16, kind="Internal").ap()
    dw_sc = nc.dram_tensor("dw_sc", [2048, 8192], BF16, kind="Internal").ap()

    es = ExitStack()
    sb = lambda n, sh, dt: es.enter_context(nc.sbuf_tensor(n, sh, dt))
    psb = lambda n, sh, dt: es.enter_context(nc.psum_tensor(n, sh, dt))

    xr_s = sb("xr_s", [128, 16 * 512], BF16)
    xn_s = sb("xn_s", [128, 16 * 512], BF16)
    xsq = [sb(f"xsq{i}", [128, 512], BF16) for i in range(2)]
    h_s = sb("h_s", [128, 16 * 512], BF16)
    gu_w = [sb(f"gu_w{i}", [128, 2 * 8192], BF16) for i in range(2)]
    dw_w = [sb(f"dw_w{i}", [128, 8 * 512], BF16) for i in range(2)]
    z_acc = sb("z_acc", [128, 4 * 2048], F32)
    zsq = sb("zsq", [128, 2048], BF16)
    o_nat = sb("o_nat", [128, 2048], F32)
    x_fin = sb("x_fin", [128, 16 * 128], F32)
    g_t = sb("g_t", [128, 4 * 512], BF16)
    u_t = sb("u_t", [128, 4 * 512], BF16)
    rows = sb("rows", [1, 4 * 512], F32)
    rows_bf = sb("rows_bf", [1, 2 * 512], BF16)
    rb = sb("rb", [128, 512], F32)
    mrb = sb("mrb", [128, 512], F32)
    zcol = sb("zcol", [128, 4], F32)
    on_c = sb("on_c", [128, 1], BF16)
    on_r = sb("on_r", [1, 128], BF16)
    id128f = sb("id128f", [128, 128], F32)
    misc_sb = sb("misc_sb", [128, 8], F32)
    scl_c = [sb(f"scl_c{i}", [128, 1], F32) for i in range(2)]

    pb = [psb(f"pb{i}", [128, 512], F32) for i in range(8)]

    S = {k: Sem(es.enter_context(nc.semaphore("sem_" + k))) for k in
         ["ld", "lx", "lq0", "lq1", "lwg0", "lwg1", "lwd0", "lwd1",
          "stq0", "stq1", "sty", "sc", "vd", "mm"]}

    NROW = 192
    NSLAB = 8
    NCHUNK = 16

    mm, sc, vd = Sched(), Sched(), Sched()

    lq_pair = lambda i: 32 * (i // 2 + 1)
    stq_val = lambda i: 16 * (i // 2 + 1)
    lx_val = lambda s: 256 * (s + 1)
    lwg_val = lambda lin: 512 * (lin // 2 + 1)
    lwd_val = lambda lin: 128 * (lin // 2 + 1)
    ld_const = 64
    ld_xf = lambda idx: 64 + 256 * (idx + 1)
    sty_val = lambda idx: 16 * (idx + 1)

    for i in range(NROW):
        sc.add("dq_bias", i)
        vd.add("dq_scale", i)

    for s in range(NSLAB):
        for k in range(16):
            sc.add("sq", s, k)
            mm.add("sqmm", s, k)
        mm.add("stats", s)
        vd.add("rows1", s)
        sc.add("rows2", s)
        vd.add("rows3", s)
        mm.add("bcast", s)
        vd.add("xn", s)
        for q4 in range(4):
            for g_ in range(4):
                g = q4 * 4 + g_
                mm.add("chunk", s, g)
                sc.add("silu", s, g)
                vd.add("gu", s, g)
            for hc in range(4):
                for icc in range(2):
                    mm.add("dicc", s, q4, hc, icc)
                vd.add("dz", s, q4, hc)
        for tt in range(4):
            sc.add("zsq", s, tt)
            vd.add("zrow", s, tt)
            sc.add("zrstd", s, tt)
            for gg in range(4):
                mm.add("xpg", s, tt, gg)
                vd.add("xev", s, tt, gg)
            vd.add("yfin", s, tt)

    with es:
        def rowinfo(i):
            if i < 64:
                return (gwq[i * 128:(i + 1) * 128, :], gsc[i * 128:(i + 1) * 128],
                        gw_sc[i * 128:(i + 1) * 128, :])
            if i < 128:
                j = i - 64
                return (uwq[j * 128:(j + 1) * 128, :], usc[j * 128:(j + 1) * 128],
                        uw_sc[j * 128:(j + 1) * 128, :])
            j = i - 128
            r, p = divmod(j, 4)
            return (dwq[r * 128:(r + 1) * 128, p * 2048:(p + 1) * 2048],
                    dsc[r * 128:(r + 1) * 128],
                    dw_sc[r * 128:(r + 1) * 128, p * 2048:(p + 1) * 2048])

        dqi = lambda b: gu_w[b][:, 0:2048]
        dqm = lambda b: gu_w[b][:, 2048:4096]
        dqo = lambda b: gu_w[b][:, 4096:6144]

        with nc.Block() as blk:
            @blk.gpsimd
            def _(q):
                for i in range(NROW):
                    src, scl, dst = rowinfo(i)
                    b = i % 2
                    if i >= 2:
                        q.wait_ge(S["vd"].h, vd["dq_scale", i - 2])
                    q.dma_start(out=dqi(b), in_=src).then_inc(S["lq%d" % b].h, 16)
                    q.dma_start(out=scl_c[b][:], in_=scl.rearrange("(p o) -> p o", o=1)).then_inc(S["lq%d" % b].h, 16)

            @blk.scalar
            def _(q):
                for i in range(NROW):
                    b = i % 2
                    q.wait_ge(S["lq%d" % b].h, lq_pair(i))
                    if i >= 2:
                        q.wait_ge(S["vd"].h, vd["dq_scale", i - 2])
                    q.activation(dqm(b), dqi(b), AF.Copy, bias=-128.0).then_inc(S["sc"].h, 1)

            @blk.vector
            def _(q):
                for i in range(NROW):
                    b = i % 2
                    q.wait_ge(S["lq%d" % b].h, lq_pair(i))
                    q.wait_ge(S["sc"].h, sc["dq_bias", i])
                    if i >= 2:
                        q.wait_ge(S["stq%d" % b].h, stq_val(i - 2))
                    q.tensor_scalar(dqo(b), dqm(b), scl_c[b][:], None, ALU.mult).then_inc(S["vd"].h, 1)

            @blk.sync
            def _(q):
                q.dma_start(out=on_c[:], in_=C["c_ones_col"]).then_inc(S["ld"].h, 16)
                q.dma_start(out=on_r[:], in_=C["c_ones_row"]).then_inc(S["ld"].h, 16)
                q.dma_start(out=id128f[:], in_=C["c_id128f"]).then_inc(S["ld"].h, 16)
                q.dma_start(out=misc_sb[:], in_=C["c_misc"]).then_inc(S["ld"].h, 16)
                for i in range(NROW):
                    _, _, dst = rowinfo(i)
                    b = i % 2
                    q.wait_ge(S["vd"].h, vd["dq_scale", i])
                    q.dma_start(out=dst, in_=dqo(b)).then_inc(S["stq%d" % b].h, 16)

        for s in range(NSLAB):
            with nc.Block() as blk:
                @blk.gpsimd
                def _(q, s=s):
                    if s >= 1:
                        q.wait_ge(S["mm"].h, mm["stats", s - 1])
                        q.wait_ge(S["vd"].h, vd["xn", s - 1])
                    for k in range(16):
                        q.dma_start(out=xr_s[:, k * 512:(k + 1) * 512],
                                    in_=xT[k * 128:(k + 1) * 128, s * 512:(s + 1) * 512]).then_inc(S["lx"].h, 16)
                    for tt in range(4):
                        q.wait_ge(S["vd"].h, vd["yfin", s, tt])
                        gt = s * 512 + tt * 128
                        bi, t0 = divmod(gt, SEQ)
                        q.dma_start(out=y[bi, t0:t0 + 128, :], in_=o_nat[:]).then_inc(S["sty"].h, 16)

                @blk.sync
                def _(q, s=s):
                    if s == 0:
                        q.wait_ge(S["stq0"].h, stq_val(NROW - 2))
                        q.wait_ge(S["stq1"].h, stq_val(NROW - 1))
                    for q4 in range(4):
                        for g_ in range(4):
                            g = q4 * 4 + g_
                            b = g % 2
                            lin = s * NCHUNK + g
                            if lin >= 2:
                                s2, g2 = divmod(lin - 2, NCHUNK)
                                q.wait_ge(S["mm"].h, mm["chunk", s2, g2])
                            for k in range(16):
                                q.dma_start_transpose(
                                    gu_w[b][:, k * 512:(k + 1) * 512],
                                    gw_sc[g * 512:(g + 1) * 512, k * 128:(k + 1) * 128]).then_inc(S["lwg%d" % b].h, 16)
                            for k in range(16):
                                q.dma_start_transpose(
                                    gu_w[b][:, 8192 + k * 512:8192 + (k + 1) * 512],
                                    uw_sc[g * 512:(g + 1) * 512, k * 128:(k + 1) * 128]).then_inc(S["lwg%d" % b].h, 16)
                        for hc in range(4):
                            for icc in range(2):
                                b = icc % 2
                                lin = ((s * 4 + q4) * 4 + hc) * 2 + icc
                                if lin >= 2:
                                    l2 = lin - 2
                                    s2, r = divmod(l2, 32)
                                    q2, r = divmod(r, 8)
                                    hc2, icc2 = divmod(r, 2)
                                    q.wait_ge(S["mm"].h, mm["dicc", s2, q2, hc2, icc2])
                                for j in range(8):
                                    ic = q4 * 16 + icc * 8 + j
                                    q.dma_start_transpose(
                                        dw_w[b][:, j * 512:(j + 1) * 512],
                                        dw_sc[hc * 512:(hc + 1) * 512, ic * 128:(ic + 1) * 128]).then_inc(S["lwd%d" % b].h, 16)
                    for tt in range(4):
                        if s >= 1 or tt >= 1:
                            s2, t2 = (s, tt - 1) if tt >= 1 else (s - 1, 3)
                            q.wait_ge(S["mm"].h, mm["xpg", s2, t2, 3])
                        for k in range(16):
                            q.dma_start(out=x_fin[:, k * 128:(k + 1) * 128],
                                        in_=xT[k * 128:(k + 1) * 128,
                                               s * 512 + tt * 128: s * 512 + (tt + 1) * 128]).then_inc(S["ld"].h, 16)

                @blk.scalar
                def _(q, s=s):
                    for k in range(16):
                        q.wait_ge(S["lx"].h, lx_val(s))
                        if k >= 2:
                            q.wait_ge(S["mm"].h, mm["sqmm", s, k - 2])
                        elif s >= 1:
                            q.wait_ge(S["mm"].h, mm["sqmm", s - 1, 14 + k])
                        q.activation(xsq[k % 2][:], xr_s[:, k * 512:(k + 1) * 512], AF.Square).then_inc(S["sc"].h, 1)
                    q.wait_ge(S["vd"].h, vd["rows1", s])
                    q.activation(rows[:, 512:1024], rows[:, 512:1024], AF.Ln, bias=misc_sb[0:1, 0:1])
                    q.drain()
                    q.activation(rows[:, 512:1024], rows[:, 512:1024], AF.Exp, scale=-0.5).then_inc(S["sc"].h, 1)
                    for g in range(NCHUNK):
                        q.wait_ge(S["mm"].h, mm["chunk", s, g])
                        for it in range(4):
                            ins = q.activation(g_t[:, it * 512:(it + 1) * 512], pb[it][:], AF.Silu)
                        ins.then_inc(S["sc"].h, 1)
                    for tt in range(4):
                        q.wait_ge(S["vd"].h, vd["dz", s, 3, 3])
                        if tt >= 1:
                            q.wait_ge(S["vd"].h, vd["zrow", s, tt - 1])
                        q.activation(zsq[:], z_acc[:, tt * 2048:(tt + 1) * 2048], AF.Square).then_inc(S["sc"].h, 1)
                        q.wait_ge(S["vd"].h, vd["zrow", s, tt])
                        q.activation(zcol[:, 1:2], zcol[:, 1:2], AF.Ln, bias=misc_sb[:, 0:1])
                        q.drain()
                        q.activation(zcol[:, 1:2], zcol[:, 1:2], AF.Exp, scale=-0.5).then_inc(S["sc"].h, 1)

                @blk.vector
                def _(q, s=s):
                    q.wait_ge(S["mm"].h, mm["stats", s])
                    q.tensor_scalar(rows[:, 0:512], pb[6][0:1, :], 1.0 / 2048, None, ALU.mult)
                    q.tensor_scalar(rows[:, 512:1024], pb[7][0:1, :], 1.0 / 2048, None, ALU.mult)
                    q.drain()
                    q.tensor_tensor(rows[:, 1024:1536], rows[:, 0:512], rows[:, 0:512], ALU.mult)
                    q.drain()
                    q.tensor_tensor(rows[:, 512:1024], rows[:, 512:1024], rows[:, 1024:1536], ALU.subtract).then_inc(S["vd"].h, 1)
                    q.wait_ge(S["sc"].h, sc["rows2", s])
                    q.drain()
                    q.tensor_copy(rows_bf[:, 0:512], rows[:, 512:1024])
                    q.tensor_tensor(rows[:, 1536:2048], rows[:, 0:512], rows[:, 512:1024], ALU.mult)
                    q.drain()
                    q.tensor_copy(rows_bf[:, 512:1024], rows[:, 1536:2048]).then_inc(S["vd"].h, 1)
                    q.wait_ge(S["mm"].h, mm["bcast", s])
                    q.tensor_copy(rb[:], pb[6][:])
                    q.tensor_copy(mrb[:], pb[7][:])
                    q.drain()
                    for k in range(16):
                        q.tensor_tensor(xn_s[:, k * 512:(k + 1) * 512], xr_s[:, k * 512:(k + 1) * 512], rb[:], ALU.mult)
                        q.drain()
                        ins = q.tensor_tensor(xn_s[:, k * 512:(k + 1) * 512], xn_s[:, k * 512:(k + 1) * 512], mrb[:], ALU.subtract)
                    ins.then_inc(S["vd"].h, 1)
                    for q4 in range(4):
                        for g_ in range(4):
                            g = q4 * 4 + g_
                            q.wait_ge(S["mm"].h, mm["chunk", s, g])
                            q.wait_ge(S["sc"].h, sc["silu", s, g])
                            for it in range(4):
                                q.tensor_copy(u_t[:, it * 512:(it + 1) * 512], pb[4 + it][:])
                            q.drain()
                            for it in range(4):
                                icL = g_ * 4 + it
                                ins = q.tensor_tensor(h_s[:, icL * 512:(icL + 1) * 512],
                                                      g_t[:, it * 512:(it + 1) * 512],
                                                      u_t[:, it * 512:(it + 1) * 512], ALU.mult)
                            ins.then_inc(S["vd"].h, 1)
                        for hc in range(4):
                            q.wait_ge(S["mm"].h, mm["dicc", s, q4, hc, 1])
                            q.drain()
                            for tt in range(4):
                                dst = z_acc[:, tt * 2048 + hc * 512: tt * 2048 + (hc + 1) * 512]
                                if q4 == 0:
                                    ins = q.tensor_copy(dst, pb[2 + tt][:])
                                else:
                                    ins = q.tensor_tensor(dst, dst, pb[2 + tt][:], ALU.add)
                            ins.then_inc(S["vd"].h, 1)
                    for tt in range(4):
                        zt = z_acc[:, tt * 2048:(tt + 1) * 2048]
                        q.wait_ge(S["sc"].h, sc["zsq", s, tt])
                        q.drain()
                        q.tensor_reduce(zcol[:, 0:1], zt, AXX, ALU.add)
                        q.tensor_reduce(zcol[:, 1:2], zsq[:], AXX, ALU.add)
                        q.drain()
                        q.tensor_scalar(zcol[:, 0:1], zcol[:, 0:1], 1.0 / 2048, None, ALU.mult)
                        q.tensor_scalar(zcol[:, 1:2], zcol[:, 1:2], 1.0 / 2048, None, ALU.mult)
                        q.drain()
                        q.tensor_tensor(zcol[:, 2:3], zcol[:, 0:1], zcol[:, 0:1], ALU.mult)
                        q.drain()
                        q.tensor_tensor(zcol[:, 1:2], zcol[:, 1:2], zcol[:, 2:3], ALU.subtract).then_inc(S["vd"].h, 1)
                        if s * 4 + tt >= 1:
                            q.wait_ge(S["sty"].h, sty_val(s * 4 + tt - 1))
                        for gg in range(4):
                            q.wait_ge(S["mm"].h, mm["xpg", s, tt, gg])
                            q.tensor_copy(o_nat[:, gg * 512:(gg + 1) * 512], pb[gg % 2][:]).then_inc(S["vd"].h, 1)
                        q.wait_ge(S["sc"].h, sc["zrstd", s, tt])
                        q.drain()
                        q.tensor_scalar(zt, zt, zcol[:, 0:1], zcol[:, 1:2], ALU.subtract, ALU.mult)
                        q.drain()
                        q.tensor_tensor(o_nat[:], o_nat[:], zt, ALU.add).then_inc(S["vd"].h, 1)

                @blk.tensor
                def _(q, s=s):
                    if s == 0:
                        q.wait_ge(S["ld"].h, ld_const)
                    q.wait_ge(S["lx"].h, lx_val(s))
                    for k in range(16):
                        q.matmul(pb[6][0:1, :], on_c[:], xr_s[:, k * 512:(k + 1) * 512],
                                 start=(k == 0), stop=(k == 15))
                        q.wait_ge(S["sc"].h, sc["sq", s, k])
                        q.matmul(pb[7][0:1, :], on_c[:], xsq[k % 2][:],
                                 start=(k == 0), stop=(k == 15)).then_inc(S["mm"].h, 1 if k < 15 else 2)
                    q.wait_ge(S["vd"].h, vd["rows3", s])
                    q.matmul(pb[6][:], on_r[:], rows_bf[0:1, 0:512], start=True, stop=True)
                    q.matmul(pb[7][:], on_r[:], rows_bf[0:1, 512:1024], start=True, stop=True).then_inc(S["mm"].h, 1)
                    q.wait_ge(S["vd"].h, vd["xn", s])
                    for q4 in range(4):
                        for g_ in range(4):
                            g = q4 * 4 + g_
                            b = g % 2
                            lin = s * NCHUNK + g
                            q.wait_ge(S["lwg%d" % (g % 2)].h, lwg_val(lin))
                            if g == 0:
                                if s >= 1:
                                    q.wait_ge(S["vd"].h, vd["yfin", s - 1, 3])
                            else:
                                q.wait_ge(S["vd"].h, vd["gu", s, g - 1])
                            if g_ == 0 and q4 >= 1:
                                q.wait_ge(S["vd"].h, vd["dz", s, q4 - 1, 3])
                            for it in range(4):
                                for k in range(16):
                                    q.matmul(pb[it][:],
                                             gu_w[b][:, k * 512 + it * 128: k * 512 + (it + 1) * 128],
                                             xn_s[:, k * 512:(k + 1) * 512],
                                             start=(k == 0), stop=(k == 15))
                            for it in range(4):
                                for k in range(16):
                                    ins = q.matmul(pb[4 + it][:],
                                                   gu_w[b][:, 8192 + k * 512 + it * 128: 8192 + k * 512 + (it + 1) * 128],
                                                   xn_s[:, k * 512:(k + 1) * 512],
                                                   start=(k == 0), stop=(k == 15))
                            ins.then_inc(S["mm"].h, 1)
                        for hc in range(4):
                            if hc == 0:
                                q.wait_ge(S["vd"].h, vd["gu", s, q4 * 4 + 3])
                            else:
                                q.wait_ge(S["vd"].h, vd["dz", s, q4, hc - 1])
                            for icc in range(2):
                                b = icc % 2
                                lind = ((s * 4 + q4) * 4 + hc) * 2 + icc
                                q.wait_ge(S["lwd%d" % (icc % 2)].h, lwd_val(lind))
                                for j in range(8):
                                    icL = icc * 8 + j
                                    for tt in range(4):
                                        ins = q.matmul(pb[2 + tt][:],
                                                       h_s[:, icL * 512 + tt * 128: icL * 512 + (tt + 1) * 128],
                                                       dw_w[b][:, j * 512:(j + 1) * 512],
                                                       start=(icc == 0 and j == 0), stop=(icc == 1 and j == 7))
                                ins.then_inc(S["mm"].h, 1)
                    q.wait_ge(S["vd"].h, vd["gu", s, 15])
                    for tt in range(4):
                        q.wait_ge(S["ld"].h, ld_xf(s * 4 + tt))
                        for gg in range(4):
                            lin = tt * 4 + gg
                            if lin >= 2:
                                t2, g2 = divmod(lin - 2, 4)
                                q.wait_ge(S["vd"].h, vd["xev", s, t2, g2])
                            elif s >= 1:
                                t2, g2 = divmod(lin - 2 + 16, 4)
                                q.wait_ge(S["vd"].h, vd["xev", s - 1, t2, g2])
                            for j in range(4):
                                k = gg * 4 + j
                                ins = q.transpose(pb[gg % 2][:, j * 128:(j + 1) * 128],
                                                  x_fin[:, k * 128:(k + 1) * 128], id128f[:])
                            ins.then_inc(S["mm"].h, 1)
        with nc.Block() as blk:
            @blk.gpsimd
            def _(q):
                q.wait_ge(S["stq0"].h, stq_val(NROW - 2))
                q.wait_ge(S["stq1"].h, stq_val(NROW - 1))
                q.wait_ge(S["sty"].h, sty_val(NSLAB * 4 - 1))
    return nc


def kernel(**inputs):
    raise NotImplementedError("phases under construction")




def build_phase_b():
    nc = bass.Bass("TRN2", target_bir_lowering=False)
    xT = nc.dram_tensor("xT", [D, TPC], F32, kind="ExternalInput").ap()
    agg_w = nc.dram_tensor("L0_ffn_agg_w", [D, D], F32, kind="ExternalInput").ap()
    agg_b = nc.dram_tensor("L0_ffn_agg_b", [D], F32, kind="ExternalInput").ap()
    in_g = nc.dram_tensor("L0_ffn_in_gamma", [D], F32, kind="ExternalInput").ap()
    in_b = nc.dram_tensor("L0_ffn_in_beta", [D], F32, kind="ExternalInput").ap()
    pn_g = nc.dram_tensor("L0_ffn_pn_gamma", [D], F32, kind="ExternalInput").ap()
    pn_b = nc.dram_tensor("L0_ffn_pn_beta", [D], F32, kind="ExternalInput").ap()
    d1_w = nc.dram_tensor("L0_ffn_d1_w", [2048, 8192], F32, kind="ExternalInput").ap()
    d1_b = nc.dram_tensor("L0_ffn_d1_b", [8192], F32, kind="ExternalInput").ap()
    d2_w = nc.dram_tensor("L0_ffn_d2_w", [8192, 2048], F32, kind="ExternalInput").ap()
    d2_b = nc.dram_tensor("L0_ffn_d2_b", [2048], F32, kind="ExternalInput").ap()
    bc_w = nc.dram_tensor("L0_ffn_bc_w", [D, D], F32, kind="ExternalInput").ap()
    bc_b = nc.dram_tensor("L0_ffn_bc_b", [D], F32, kind="ExternalInput").ap()
    gt_w = nc.dram_tensor("L0_ffn_gate_w", [D, D], F32, kind="ExternalInput").ap()
    gt_b = nc.dram_tensor("L0_ffn_gate_b", [D], F32, kind="ExternalInput").ap()
    out_g = nc.dram_tensor("L0_ffn_out_gamma", [D], F32, kind="ExternalInput").ap()
    out_b = nc.dram_tensor("L0_ffn_out_beta", [D], F32, kind="ExternalInput").ap()
    yT = nc.dram_tensor("yT", [D, TPC], F32, kind="ExternalOutput").ap()
    C = declare_consts(nc, ["c_ones_col", "c_ones_row", "c_misc"])

    d1_sc = nc.dram_tensor("d1_sc", [2048, 8192], BF16, kind="Internal").ap()
    d2_sc = nc.dram_tensor("d2_sc", [8192, 2048], BF16, kind="Internal").ap()

    es = ExitStack()
    sb = lambda n, sh, dt: es.enter_context(nc.sbuf_tensor(n, sh, dt))
    psb = lambda n, sh, dt: es.enter_context(nc.psum_tensor(n, sh, dt))

    gw_b = sb("gw_b", [128, 16 * 2048], BF16)
    wst = [sb(f"wst{i}", [128, 8192], BF16) for i in range(2)]
    x_s = sb("x_s", [128, 16 * 512], F32)
    xr_s = sb("xr_s", [128, 16 * 512], BF16)
    xn_s = sb("xn_s", [128, 16 * 512], BF16)
    out_s = sb("out_s", [128, 16 * 512], BF16)
    xsq = [sb(f"xsq{i}", [128, 512], BF16) for i in range(2)]
    gate_t = [sb(f"gate_t{i}", [128, 512], BF16) for i in range(2)]
    gcol = sb("gcol", [128, 16 * 6], F32)
    gbcol = sb("gbcol", [128, 16], F32)
    bcol1 = sb("bcol1", [128, 64], F32)
    bcol2 = sb("bcol2", [128, 3 * 16], F32)
    bexp1 = sb("bexp1", [128, 128], F32)
    bexp2 = sb("bexp2", [128, 3 * 32], F32)
    tb_f = sb("tb_f", [128, 128], F32)
    acc_sm = sb("acc_sm", [128, 128], F32)
    th = sb("th", [128, 32], F32)
    th_bf = sb("th_bf", [128, 2 * 32], BF16)
    tn_bf = sb("tn_bf", [128, 32], BF16)
    t1_bf = sb("t1_bf", [128, 128], BF16)
    xl = sb("xl", [128, 2 * 32], F32)
    xl_bf = sb("xl_bf", [128, 2 * 32], BF16)
    bcT = sb("bcT", [128, 32], F32)
    rows = sb("rows", [1, 3 * 512], F32)
    rows_bf = sb("rows_bf", [1, 2 * 512], BF16)
    rws = sb("rws", [1, 16], F32)
    rws_bf = sb("rws_bf", [1, 8], BF16)
    rb = sb("rb", [128, 512], F32)
    mrb = sb("mrb", [128, 512], F32)
    rbs = sb("rbs", [128, 4], F32)
    on_c = sb("on_c", [128, 1], BF16)
    on_r = sb("on_r", [1, 128], BF16)
    misc_sb = sb("misc_sb", [128, 8], F32)

    pb = [psb(f"pb{i}", [128, 512], F32) for i in range(8)]

    S = {k: Sem(es.enter_context(nc.semaphore("sem_" + k))) for k in
         ["ld", "lx", "lq0", "lq1", "lh0", "lh1", "st0", "st1", "sty", "sc", "vd", "mm", "aux"]}
    N = {k: 0 for k in S}

    def dma(q, key, out, in_):
        N[key] += 16
        return q.dma_start(out=out, in_=in_).then_inc(S[key].h, 16)

    mm, sc, vd = Sched(), Sched(), Sched()
    NSPILL = 80
    NSLAB = 8
    eps5 = misc_sb[0:1, 0:1]
    eps6 = misc_sb[0:1, 1:2]

    def small_stats(tag, src_bf, eps_col, cast_key):
        with nc.Block() as blk:
            @blk.scalar
            def _(q):
                wait(q, S["vd"], vd[cast_key])
                q.activation(src_bf[:, 32:64], src_bf[:, 0:32], AF.Square).then_inc(S["sc"].h, (sc.add(tag, "sq"), 1)[1])

            @blk.tensor
            def _(q):
                wait(q, S["sc"], sc[tag, "sq"])
                for k in range(16):
                    q.matmul(pb[6][0:1, 0:2], on_c[:], src_bf[:, k * 2:(k + 1) * 2],
                             start=(k == 0), stop=(k == 15))
                for k in range(16):
                    ins = q.matmul(pb[7][0:1, 0:2], on_c[:], src_bf[:, 32 + k * 2:32 + (k + 1) * 2],
                                   start=(k == 0), stop=(k == 15))
                ins.then_inc(S["mm"].h, (mm.add(tag, "stats"), 1)[1])

            @blk.vector
            def _(q):
                wait(q, S["mm"], mm[tag, "stats"])
                q.tensor_scalar(rws[0:1, 0:2], pb[6][0:1, 0:2], 1.0 / 2048, None, ALU.mult)
                q.tensor_scalar(rws[0:1, 2:4], pb[7][0:1, 0:2], 1.0 / 2048, None, ALU.mult)
                q.drain()
                q.tensor_tensor(rws[0:1, 4:6], rws[0:1, 0:2], rws[0:1, 0:2], ALU.mult)
                q.drain()
                q.tensor_tensor(rws[0:1, 2:4], rws[0:1, 2:4], rws[0:1, 4:6], ALU.subtract).then_inc(S["vd"].h, (vd.add(tag, "var"), 1)[1])

            @blk.scalar
            def _(q):
                wait(q, S["vd"], vd[tag, "var"])
                q.activation(rws[0:1, 2:4], rws[0:1, 2:4], AF.Ln, bias=eps_col)
                q.drain()
                q.activation(rws[0:1, 2:4], rws[0:1, 2:4], AF.Exp, scale=-0.5).then_inc(S["sc"].h, (sc.add(tag, "rstd"), 1)[1])

            @blk.vector
            def _(q):
                wait(q, S["sc"], sc[tag, "rstd"])
                q.tensor_copy(rws_bf[0:1, 0:2], rws[0:1, 2:4])
                q.tensor_tensor(rws[0:1, 6:8], rws[0:1, 0:2], rws[0:1, 2:4], ALU.mult)
                q.drain()
                q.tensor_copy(rws_bf[0:1, 2:4], rws[0:1, 6:8]).then_inc(S["vd"].h, (vd.add(tag, "rbf"), 1)[1])

            @blk.tensor
            def _(q):
                wait(q, S["vd"], vd[tag, "rbf"])
                q.matmul(pb[6][:, 0:2], on_r[:], rws_bf[0:1, 0:2], start=True, stop=True)
                q.matmul(pb[7][:, 0:2], on_r[:], rws_bf[0:1, 2:4],
                         start=True, stop=True).then_inc(S["mm"].h, (mm.add(tag, "bc"), 1)[1])

            @blk.vector
            def _(q):
                wait(q, S["mm"], mm[tag, "bc"])
                q.tensor_copy(rbs[:, 0:2], pb[6][:, 0:2])
                ins = q.tensor_copy(rbs[:, 2:4], pb[7][:, 0:2])
                ins.then_inc(S["vd"].h, (vd.add(tag, "rbc"), 1)[1])

    def small_proj(tag, wsrc, cast, nk, no_t, rhs_bf, outbank, rhs_key):
        pre = "lq" if cast else "lh"
        base = [N[pre + "0"], N[pre + "1"]]
        for k in range(nk):
            mm.add(tag, "k", k)
            vd.add(tag, "kv", k)
        with nc.Block() as blk:
            if cast:
                @blk.gpsimd
                def _(q):
                    for k in range(nk):
                        b = k % 2
                        if k >= 2:
                            wait(q, S["mm"], mm[tag, "k", k - 2])
                        dma(q, pre + "%d" % b, wst[b][:, 0:no_t * 128], wsrc(k))
            else:
                @blk.sync
                def _(q):
                    for k in range(nk):
                        b = k % 2
                        if k >= 2:
                            wait(q, S["mm"], mm[tag, "k", k - 2])
                        dma(q, pre + "%d" % b, wst[b][:, 0:no_t * 128], wsrc(k))

            @blk.tensor
            def _(q):
                wait(q, S["vd"], vd[rhs_key])
                for k in range(nk):
                    b = k % 2
                    q.wait_ge(S[pre + "%d" % b].h, base[b] + 16 * (k // 2 + 1))
                    if k >= 2:
                        wait(q, S["vd"], vd[tag, "kv", k - 2])
                    for ot in range(no_t):
                        ins = q.matmul(pb[outbank + b][:, ot * 2: ot * 2 + 2],
                                       wst[b][:, ot * 128:(ot + 1) * 128],
                                       rhs_bf[:, k * 2:(k + 1) * 2],
                                       start=True, stop=True)
                    ins.then_inc(S["mm"].h, 1)

            @blk.vector
            def _(q):
                for k in range(nk):
                    b = k % 2
                    wait(q, S["mm"], mm[tag, "k", k])
                    q.drain()
                    if k == 0:
                        ins = q.tensor_copy(acc_sm[:, 0:no_t * 2], pb[outbank][:, 0:no_t * 2])
                    else:
                        ins = q.tensor_tensor(acc_sm[:, 0:no_t * 2], acc_sm[:, 0:no_t * 2],
                                              pb[outbank + b][:, 0:no_t * 2], ALU.add)
                    ins.then_inc(S["vd"].h, 1)

    with es:
        with nc.Block() as blk:
            @blk.gpsimd
            def _(q):
                for i in range(NSPILL):
                    b = i % 2
                    if i >= 2:
                        q.wait_ge(S["st%d" % b].h, 16 * (i // 2))
                    if i < 16:
                        dma(q, "lq%d" % b, wst[b][:, 0:8192], d1_w[i * 128:(i + 1) * 128, :])
                    else:
                        dma(q, "lq%d" % b, wst[b][:, 0:2048], d2_w[(i - 16) * 128:(i - 15) * 128, :])
                for k in range(16):
                    dma(q, "lx", gw_b[:, k * 2048:(k + 1) * 2048],
                        gt_w[k * 128:(k + 1) * 128, :])

            @blk.sync
            def _(q):
                for i in range(NSPILL):
                    b = i % 2
                    q.wait_ge(S["lq%d" % b].h, 16 * (i // 2 + 1))
                    if i < 16:
                        dma(q, "st%d" % b, d1_sc[i * 128:(i + 1) * 128, :], wst[b][:, 0:8192])
                    else:
                        dma(q, "st%d" % b, d2_sc[(i - 16) * 128:(i - 15) * 128, :], wst[b][:, 0:2048])
                for ap_, dst in [(C["c_ones_col"], on_c[:]), (C["c_ones_row"], on_r[:]),
                                 (C["c_misc"], misc_sb[:])]:
                    dma(q, "ld", dst, ap_)
                for j, t in enumerate([in_g, in_b, pn_g, pn_b, out_g, out_b]):
                    for k in range(16):
                        dma(q, "ld", gcol[:, j * 16 + k: j * 16 + k + 1],
                            t[k * 128:(k + 1) * 128].rearrange("(p o) -> p o", o=1))
                for k in range(16):
                    dma(q, "ld", gbcol[:, k:k + 1],
                        gt_b[k * 128:(k + 1) * 128].rearrange("(p o) -> p o", o=1))
                with nc.allow_non_contiguous_dma(reason="last-token columns, 16x512B"):
                    for k in range(16):
                        dma(q, "ld", xl[:, k * 2: k * 2 + 1], xT[k * 128:(k + 1) * 128, 2047:2048])
                        dma(q, "ld", xl[:, k * 2 + 1: k * 2 + 2], xT[k * 128:(k + 1) * 128, 4095:4096])
                for k in range(64):
                    dma(q, "ld", bcol1[:, k:k + 1], d1_b[k * 128:(k + 1) * 128].rearrange("(p o) -> p o", o=1))
                for j, t in enumerate([d2_b, agg_b, bc_b]):
                    for k in range(16):
                        dma(q, "ld", bcol2[:, j * 16 + k: j * 16 + k + 1],
                            t[k * 128:(k + 1) * 128].rearrange("(p o) -> p o", o=1))

        ld_pre = N["ld"]
        st_spill = (N["st0"], N["st1"])

        with nc.Block() as blk:
            @blk.vector
            def _(q):
                q.wait_ge(S["ld"].h, ld_pre)
                q.wait_ge(S["lx"].h, N["lx"])
                e1 = bexp1.rearrange("p (a two) -> p a two", two=2)
                c1 = bcol1.rearrange("p (a o) -> p a o", o=1)
                q.tensor_copy(e1[:, :, 0:1], c1[:, :, 0:1])
                q.tensor_copy(e1[:, :, 1:2], c1[:, :, 0:1])
                e2 = bexp2.rearrange("p (a two) -> p a two", two=2)
                c2 = bcol2.rearrange("p (a o) -> p a o", o=1)
                q.tensor_copy(e2[:, :, 0:1], c2[:, :, 0:1])
                q.tensor_copy(e2[:, :, 1:2], c2[:, :, 0:1])
                ins = q.tensor_copy(xl_bf[:, 0:32], xl[:, 0:32])
                ins.then_inc(S["vd"].h, (vd.add("lc", "cast"), 1)[1])

        small_stats("lc1", xl_bf, eps5, ("lc", "cast"))
        with nc.Block() as blk:
            @blk.vector
            def _(q):
                wait(q, S["vd"], vd["lc1", "rbc"])
                q.drain()
                for k in range(16):
                    q.tensor_tensor(xl[:, 32 + k * 2:32 + (k + 1) * 2], xl[:, k * 2:(k + 1) * 2], rbs[:, 0:2], ALU.mult)
                q.drain()
                for k in range(16):
                    q.tensor_tensor(xl[:, 32 + k * 2:32 + (k + 1) * 2], xl[:, 32 + k * 2:32 + (k + 1) * 2], rbs[:, 2:4], ALU.subtract)
                q.drain()
                ins = q.tensor_copy(xl_bf[:, 0:32], xl[:, 32:64])
                ins.then_inc(S["vd"].h, (vd.add("lc2", "cast"), 1)[1])
        small_stats("lc2", xl_bf, eps6, ("lc2", "cast"))
        with nc.Block() as blk:
            @blk.vector
            def _(q):
                wait(q, S["vd"], vd["lc2", "rbc"])
                q.drain()
                for k in range(16):
                    q.tensor_tensor(xl[:, k * 2:(k + 1) * 2], xl[:, 32 + k * 2:32 + (k + 1) * 2], rbs[:, 0:2], ALU.mult)
                q.drain()
                for k in range(16):
                    q.tensor_tensor(xl[:, k * 2:(k + 1) * 2], xl[:, k * 2:(k + 1) * 2], rbs[:, 2:4], ALU.subtract)
                q.drain()
                for k in range(16):
                    q.tensor_scalar(xl_bf[:, k * 2:(k + 1) * 2], xl[:, k * 2:(k + 1) * 2],
                                    gcol[:, k:k + 1], gcol[:, 16 + k:16 + k + 1], ALU.mult, ALU.add)
                ins = q.engine_nop()
                ins.then_inc(S["vd"].h, (vd.add("xnn", "done"), 1)[1])
            @blk.gpsimd
            def _(q):
                q.wait_ge(S["st0"].h, st_spill[0])
                q.wait_ge(S["st1"].h, st_spill[1])
                q.engine_nop().then_inc(S["aux"].h, 1)

        with nc.Block() as blk:
            @blk.gpsimd
            def _(q):
                q.wait_ge(S["aux"].h, 1)
        small_proj("agg", lambda k: agg_w[k * 128:(k + 1) * 128, :], True,
                   16, 16, xl_bf, 0, ("xnn", "done"))
        with nc.Block() as blk:
            @blk.vector
            def _(q):
                wait(q, S["vd"], vd["agg", "kv", 15])
                q.drain()
                ins = q.tensor_tensor(th[:, 0:32], acc_sm[:, 0:32], bexp2[:, 32:64], ALU.add)
                ins.then_inc(S["vd"].h, (vd.add("th", 0), 1)[1])

        for it in range(4):
            with nc.Block() as blk:
                @blk.vector
                def _(q, it=it):
                    wait(q, S["vd"], vd["th", it])
                    q.drain()
                    ins = q.tensor_copy(th_bf[:, 0:32], th[:, 0:32])
                    ins.then_inc(S["vd"].h, (vd.add(("thc", it), "cast"), 1)[1])
            small_stats(("thst", it), th_bf, eps6, (("thc", it), "cast"))
            with nc.Block() as blk:
                @blk.vector
                def _(q, it=it):
                    wait(q, S["vd"], vd[("thst", it), "rbc"])
                    q.drain()
                    for k in range(16):
                        q.tensor_tensor(xl[:, k * 2:(k + 1) * 2], th[:, k * 2:(k + 1) * 2], rbs[:, 0:2], ALU.mult)
                    q.drain()
                    for k in range(16):
                        q.tensor_tensor(xl[:, k * 2:(k + 1) * 2], xl[:, k * 2:(k + 1) * 2], rbs[:, 2:4], ALU.subtract)
                    q.drain()
                    for k in range(16):
                        q.tensor_scalar(tn_bf[:, k * 2:(k + 1) * 2], xl[:, k * 2:(k + 1) * 2],
                                        gcol[:, 32 + k:32 + k + 1], gcol[:, 48 + k:48 + k + 1], ALU.mult, ALU.add)
                    ins = q.engine_nop()
                    ins.then_inc(S["vd"].h, (vd.add(("tn", it), "done"), 1)[1])
            small_proj(("d1", it), lambda k: d1_sc[k * 128:(k + 1) * 128, :], False,
                       16, 64, tn_bf, 0, (("tn", it), "done"))
            with nc.Block() as blk:
                @blk.vector
                def _(q, it=it):
                    wait(q, S["vd"], vd[("d1", it), "kv", 15])
                    q.drain()
                    ins = q.tensor_tensor(tb_f[:, 0:128], acc_sm[:, 0:128], bexp1[:, 0:128], ALU.add)
                    ins.then_inc(S["vd"].h, (vd.add(("d1b", it)), 1)[1])
                @blk.scalar
                def _(q, it=it):
                    wait(q, S["vd"], vd[("d1b", it)])
                    ins = q.activation(t1_bf[:, 0:128], tb_f[:, 0:128], AF.Gelu)
                    ins.then_inc(S["sc"].h, (sc.add(("gelu", it)), 1)[1])
                @blk.vector
                def _(q, it=it):
                    wait(q, S["sc"], sc[("gelu", it)])
                    q.engine_nop().then_inc(S["vd"].h, (vd.add(("t1", it), "done"), 1)[1])
            small_proj(("d2", it), lambda k: d2_sc[k * 128:(k + 1) * 128, :], False,
                       64, 16, t1_bf, 1, (("t1", it), "done"))
            with nc.Block() as blk:
                @blk.vector
                def _(q, it=it):
                    wait(q, S["vd"], vd[("d2", it), "kv", 63])
                    q.drain()
                    q.tensor_tensor(th[:, 0:32], th[:, 0:32], acc_sm[:, 0:32], ALU.add)
                    q.drain()
                    ins = q.tensor_tensor(th[:, 0:32], th[:, 0:32], bexp2[:, 0:32], ALU.add)
                    ins.then_inc(S["vd"].h, (vd.add("th", it + 1), 1)[1])

        with nc.Block() as blk:
            @blk.vector
            def _(q):
                wait(q, S["vd"], vd["th", 4])
                q.drain()
                ins = q.tensor_copy(th_bf[:, 0:32], th[:, 0:32])
                ins.then_inc(S["vd"].h, (vd.add("thf", "cast"), 1)[1])
        small_proj("bc", lambda k: bc_w[k * 128:(k + 1) * 128, :], True,
                   16, 16, th_bf, 2, ("thf", "cast"))
        with nc.Block() as blk:
            @blk.vector
            def _(q):
                wait(q, S["vd"], vd["bc", "kv", 15])
                q.drain()
                ins = q.tensor_tensor(bcT[:, 0:32], acc_sm[:, 0:32], bexp2[:, 64:96], ALU.add)
                ins.then_inc(S["vd"].h, (vd.add("bcT", "done"), 1)[1])

        def reg_b1(s):
            for k in range(16):
                mm.add(("b1sq1", s), k)
            mm.at[(("b1st1", s), "stats")] = mm.count
            mm.add(("b1bc1", s))
            for ot in range(16):
                mm.add(("b1g", s), ot)
            for k in range(16):
                mm.add(("b1sq2", s), k)
            mm.at[(("b1st2", s), "stats")] = mm.count
            mm.add(("b1bc2", s))
            for k in range(16):
                mm.add(("b1sq3", s), k)
            mm.at[(("b1st3", s), "stats")] = mm.count
            mm.add(("b1bc3", s))
            for k in range(16):
                sc.add(("b1q1", s), k)
            sc.add("b1rs1", s)
            for ot in range(16):
                sc.add(("b1sig", s), ot)
            for k in range(16):
                sc.add(("b1q2", s), k)
            sc.add("b1rs2", s)
            for k in range(16):
                sc.add(("b1q3", s), k)
            sc.add("b1rs3", s)
            vd.add("b1r1", s)
            vd.add(("b1bf1", s))
            vd.add("b1xn", s)
            for ot in range(16):
                vd.add(("b1out", s), ot)
            vd.add("b1r2", s)
            vd.add(("b1bf2", s))
            vd.add("b1fo", s)
            vd.add("b1r3", s)
            vd.add(("b1bf3", s))
            vd.add("b1fin", s)

        for s in range(NSLAB):
            reg_b1(s)
            lx_base = N["lx"]
            ld_base = N["ld"]
            with nc.Block() as blk:
                @blk.gpsimd
                def _(q, s=s):
                    if s >= 1:
                        wait(q, S["mm"], mm[("b1st1", s - 1), "stats"])
                        wait(q, S["vd"], vd["b1xn", s - 1])
                    for k in range(16):
                        dma(q, "lx", xr_s[:, k * 512:(k + 1) * 512],
                            xT[k * 128:(k + 1) * 128, s * 512:(s + 1) * 512])
                    wait(q, S["vd"], vd.get("b1fin_prev_placeholder", 0) if False else 0)

                @blk.sync
                def _(q, s=s):
                    if s >= 1:
                        wait(q, S["vd"], vd["b1fin", s - 1])
                        q.wait_ge(S["sty"].h, 256 * s)
                    for k in range(16):
                        dma(q, "ld", x_s[:, k * 512:(k + 1) * 512],
                            xT[k * 128:(k + 1) * 128, s * 512:(s + 1) * 512])

                @blk.scalar
                def _(q, s=s):
                    q.wait_ge(S["lx"].h, lx_base + 256)
                    for k in range(16):
                        if k >= 2:
                            wait(q, S["mm"], mm[("b1sq1", s), k - 2])
                        elif s >= 1:
                            wait(q, S["mm"], mm[("b1sq3", s - 1), 14 + k])
                        q.activation(xsq[k % 2][:], xr_s[:, k * 512:(k + 1) * 512], AF.Square).then_inc(S["sc"].h, 1)
                    wait(q, S["vd"], vd["b1r1", s])
                    q.activation(rows[:, 512:1024], rows[:, 512:1024], AF.Ln, bias=eps5)
                    q.drain()
                    q.activation(rows[:, 512:1024], rows[:, 512:1024], AF.Exp, scale=-0.5).then_inc(S["sc"].h, 1)
                    for ot in range(16):
                        wait(q, S["mm"], mm[("b1g", s), ot])
                        if ot >= 2:
                            wait(q, S["vd"], vd[("b1out", s), ot - 2])
                        q.activation(gate_t[ot % 2][:], pb[ot % 2][:], AF.Sigmoid,
                                     bias=gbcol[:, ot:ot + 1]).then_inc(S["sc"].h, 1)
                    for k in range(16):
                        if k >= 2:
                            wait(q, S["mm"], mm[("b1sq2", s), k - 2])
                        q.activation(xsq[k % 2][:], out_s[:, k * 512:(k + 1) * 512], AF.Square).then_inc(S["sc"].h, 1)
                    wait(q, S["vd"], vd["b1r2", s])
                    q.activation(rows[:, 512:1024], rows[:, 512:1024], AF.Ln, bias=eps6)
                    q.drain()
                    q.activation(rows[:, 512:1024], rows[:, 512:1024], AF.Exp, scale=-0.5).then_inc(S["sc"].h, 1)
                    wait(q, S["vd"], vd["b1fo", s])
                    for k in range(16):
                        if k >= 2:
                            wait(q, S["mm"], mm[("b1sq3", s), k - 2])
                        q.activation(xsq[k % 2][:], out_s[:, k * 512:(k + 1) * 512], AF.Square).then_inc(S["sc"].h, 1)
                    wait(q, S["vd"], vd["b1r3", s])
                    q.activation(rows[:, 512:1024], rows[:, 512:1024], AF.Ln, bias=eps5)
                    q.drain()
                    q.activation(rows[:, 512:1024], rows[:, 512:1024], AF.Exp, scale=-0.5).then_inc(S["sc"].h, 1)

                def emit_rows1(q, tag_rows, tag_out):
                    q.tensor_scalar(rows[:, 0:512], pb[6][0:1, :], 1.0 / 2048, None, ALU.mult)
                    q.tensor_scalar(rows[:, 512:1024], pb[7][0:1, :], 1.0 / 2048, None, ALU.mult)
                    q.drain()
                    q.tensor_tensor(rows[:, 1024:1536], rows[:, 0:512], rows[:, 0:512], ALU.mult)
                    q.drain()
                    q.tensor_tensor(rows[:, 512:1024], rows[:, 512:1024], rows[:, 1024:1536], ALU.subtract).then_inc(S["vd"].h, 1)

                def emit_rows3(q, sc_tag, vd_tag):
                    wait(q, S["sc"], sc[sc_tag])
                    q.drain()
                    q.tensor_copy(rows_bf[:, 0:512], rows[:, 512:1024])
                    q.tensor_tensor(rows[:, 1024:1536], rows[:, 0:512], rows[:, 512:1024], ALU.mult)
                    q.drain()
                    q.tensor_copy(rows_bf[:, 512:1024], rows[:, 1024:1536]).then_inc(S["vd"].h, 1)

                @blk.vector
                def _(q, s=s, emit_rows1=emit_rows1, emit_rows3=emit_rows3):
                    wait(q, S["mm"], mm[("b1st1", s), "stats"])
                    emit_rows1(q, "b1r1", s)
                    emit_rows3(q, ("b1rs1", s), ("b1bf1", s))
                    wait(q, S["mm"], mm[("b1bc1", s)])
                    q.tensor_copy(rb[:], pb[6][:])
                    q.tensor_copy(mrb[:], pb[7][:])
                    q.drain()
                    for k in range(16):
                        q.tensor_tensor(xn_s[:, k * 512:(k + 1) * 512], xr_s[:, k * 512:(k + 1) * 512], rb[:], ALU.mult)
                        q.drain()
                        ins = q.tensor_tensor(xn_s[:, k * 512:(k + 1) * 512], xn_s[:, k * 512:(k + 1) * 512], mrb[:], ALU.subtract)
                    ins.then_inc(S["vd"].h, 1)
                    bcol = 0 if s < 4 else 1
                    for ot in range(16):
                        wait(q, S["sc"], sc[("b1sig", s), ot])
                        ins = q.scalar_tensor_tensor(out_s[:, ot * 512:(ot + 1) * 512],
                                                     gate_t[ot % 2][:], bcT[:, ot * 2 + bcol: ot * 2 + bcol + 1],
                                                     xn_s[:, ot * 512:(ot + 1) * 512], ALU.mult, ALU.add)
                        ins.then_inc(S["vd"].h, 1)
                    wait(q, S["mm"], mm[("b1st2", s), "stats"])
                    emit_rows1(q, "b1r2", s)
                    emit_rows3(q, ("b1rs2", s), ("b1bf2", s))
                    wait(q, S["mm"], mm[("b1bc2", s)])
                    q.tensor_copy(rb[:], pb[6][:])
                    q.tensor_copy(mrb[:], pb[7][:])
                    q.drain()
                    for k in range(16):
                        q.tensor_tensor(out_s[:, k * 512:(k + 1) * 512], out_s[:, k * 512:(k + 1) * 512], rb[:], ALU.mult)
                        q.drain()
                        q.tensor_tensor(out_s[:, k * 512:(k + 1) * 512], out_s[:, k * 512:(k + 1) * 512], mrb[:], ALU.subtract)
                        q.drain()
                        ins = q.tensor_scalar(out_s[:, k * 512:(k + 1) * 512], out_s[:, k * 512:(k + 1) * 512],
                                              gcol[:, 64 + k:64 + k + 1], gcol[:, 80 + k:80 + k + 1], ALU.mult, ALU.add)
                    ins.then_inc(S["vd"].h, 1)
                    wait(q, S["mm"], mm[("b1st3", s), "stats"])
                    emit_rows1(q, "b1r3", s)
                    emit_rows3(q, ("b1rs3", s), ("b1bf3", s))
                    wait(q, S["mm"], mm[("b1bc3", s)])
                    q.tensor_copy(rb[:], pb[6][:])
                    q.tensor_copy(mrb[:], pb[7][:])
                    q.wait_ge(S["ld"].h, ld_base + 256)
                    if s >= 1:
                        q.wait_ge(S["sty"].h, 256 * s)
                    q.drain()
                    for k in range(16):
                        q.tensor_tensor(xsq[0][:], out_s[:, k * 512:(k + 1) * 512], rb[:], ALU.mult)
                        q.drain()
                        q.tensor_tensor(xsq[0][:], xsq[0][:], mrb[:], ALU.subtract)
                        q.drain()
                        ins = q.tensor_tensor(x_s[:, k * 512:(k + 1) * 512], x_s[:, k * 512:(k + 1) * 512], xsq[0][:], ALU.add)
                        q.drain()
                    ins.then_inc(S["vd"].h, 1)

                @blk.tensor
                def _(q, s=s):
                    if s == 0:
                        wait(q, S["vd"], vd["bcT", "done"])
                    q.wait_ge(S["lx"].h, lx_base + 256)
                    for k in range(16):
                        q.matmul(pb[6][0:1, :], on_c[:], xr_s[:, k * 512:(k + 1) * 512],
                                 start=(k == 0), stop=(k == 15))
                        wait(q, S["sc"], sc[("b1q1", s), k])
                        q.matmul(pb[7][0:1, :], on_c[:], xsq[k % 2][:],
                                 start=(k == 0), stop=(k == 15)).then_inc(S["mm"].h, 1)
                    wait(q, S["vd"], vd[("b1bf1", s)])
                    q.matmul(pb[6][:], on_r[:], rows_bf[0:1, 0:512], start=True, stop=True)
                    q.matmul(pb[7][:], on_r[:], rows_bf[0:1, 512:1024],
                             start=True, stop=True).then_inc(S["mm"].h, 1)
                    wait(q, S["vd"], vd["b1xn", s])
                    for ot in range(16):
                        if ot >= 2:
                            wait(q, S["sc"], sc[("b1sig", s), ot - 2])
                        for k in range(16):
                            ins = q.matmul(pb[ot % 2][:], gw_b[:, k * 2048 + ot * 128: k * 2048 + (ot + 1) * 128],
                                           xn_s[:, k * 512:(k + 1) * 512], start=(k == 0), stop=(k == 15))
                        ins.then_inc(S["mm"].h, 1)
                    for k in range(16):
                        wait(q, S["vd"], vd[("b1out", s), k])
                        q.matmul(pb[6][0:1, :], on_c[:], out_s[:, k * 512:(k + 1) * 512],
                                 start=(k == 0), stop=(k == 15))
                        wait(q, S["sc"], sc[("b1q2", s), k])
                        q.matmul(pb[7][0:1, :], on_c[:], xsq[k % 2][:],
                                 start=(k == 0), stop=(k == 15)).then_inc(S["mm"].h, 1)
                    wait(q, S["vd"], vd[("b1bf2", s)])
                    q.matmul(pb[6][:], on_r[:], rows_bf[0:1, 0:512], start=True, stop=True)
                    q.matmul(pb[7][:], on_r[:], rows_bf[0:1, 512:1024],
                             start=True, stop=True).then_inc(S["mm"].h, 1)
                    wait(q, S["vd"], vd["b1fo", s])
                    for k in range(16):
                        q.matmul(pb[6][0:1, :], on_c[:], out_s[:, k * 512:(k + 1) * 512],
                                 start=(k == 0), stop=(k == 15))
                        wait(q, S["sc"], sc[("b1q3", s), k])
                        q.matmul(pb[7][0:1, :], on_c[:], xsq[k % 2][:],
                                 start=(k == 0), stop=(k == 15)).then_inc(S["mm"].h, 1)
                    wait(q, S["vd"], vd[("b1bf3", s)])
                    q.matmul(pb[6][:], on_r[:], rows_bf[0:1, 0:512], start=True, stop=True)
                    q.matmul(pb[7][:], on_r[:], rows_bf[0:1, 512:1024],
                             start=True, stop=True).then_inc(S["mm"].h, 1)

            with nc.Block() as blk:
                @blk.gpsimd
                def _(q, s=s):
                    wait(q, S["vd"], vd["b1fin", s])
                    for k in range(16):
                        dma(q, "sty", yT[k * 128:(k + 1) * 128, s * 512:(s + 1) * 512],
                            x_s[:, k * 512:(k + 1) * 512])

        with nc.Block() as blk:
            @blk.gpsimd
            def _(q):
                q.wait_ge(S["sty"].h, N["sty"])
    return nc




def _PHASE_C_CONT(nc, blk, q_args):
    s = q_args["s"]; bi = q_args["bi"]; sl = q_args["sl"]; tok0 = q_args["tok0"]
    lx_base = q_args["lx_base"]; ld_base = q_args["ld_base"]
    S = q_args["S"]; N = q_args["N"]
    mm = q_args["mm"]; sc = q_args["sc"]; vd = q_args["vd"]
    loadrec = q_args["loadrec"]
    B = q_args["bufs"]
    PAD, QSL = 4, 516
    PROJS = [("q", 12), ("k", 12), ("v", 24), ("g", 24)]
    pb = B["pb"]

    def inc(ins, key):
        return ins.then_inc(S[key].h, 1)

    LAD = [("tra", 0), ("p1t", 0), ("p1t", 1)]
    for _L in (0, 1):
        for _lv in range(4):
            LAD += [("sqp", _L), ("prd", _L)]
        LAD += [("sq1", _L), ("prd", _L), ("tfin", _L)]
    LAD += [("apa", 0), ("off", 0), ("apb", 1)]

    @blk.vector
    def _(q):
        xn_s = B["xn_s"]; rows = B["rows"]; rows_bf = B["rows_bf"]
        rb = B["rb"]; mrb = B["mrb"]
        if sl == 0:
            q.memset(B["S_f"][:], 0.0)
            q.memset(B["sb_t"][:], 0.0)
            q.memset(B["gcar"][:], 0.0)
            for j in range(48):
                buf = (B["qpre"] if j < 12 else B["kpre"] if j < 24 else B["vpre"])
                jj = j if j < 12 else j - 12 if j < 24 else j - 24
                q.memset(buf[:, jj * QSL + 1: jj * QSL + PAD], 0.0)
            q.drain()
        q.wait_ge(S["mm"].h, mm[("csq", s), 15])
        q.tensor_scalar(rows[:, 0:512], pb[6][0:1, :], 1.0 / 2048, None, ALU.mult)
        q.tensor_scalar(rows[:, 512:1024], pb[7][0:1, :], 1.0 / 2048, None, ALU.mult)
        q.drain()
        q.tensor_tensor(B["l2r"][0:1, :], rows[:, 0:512], rows[:, 0:512], ALU.mult)
        q.drain()
        inc(q.tensor_tensor(rows[:, 512:1024], rows[:, 512:1024], B["l2r"][0:1, :], ALU.subtract), "vd")
        wait(q, S["sc"], sc["crs1", s])
        q.drain()
        q.tensor_copy(rows_bf[:, 0:512], rows[:, 512:1024])
        q.tensor_tensor(B["l2r"][0:1, :], rows[:, 0:512], rows[:, 512:1024], ALU.mult)
        q.drain()
        inc(q.tensor_copy(rows_bf[:, 512:1024], B["l2r"][0:1, :]), "vd")
        q.wait_ge(S["mm"].h, mm["cbc1", s])
        q.tensor_copy(rb[:], pb[6][:])
        q.tensor_copy(mrb[:], pb[7][:])
        q.drain()
        for k in range(16):
            q.tensor_tensor(B["cvt"][:], xn_s[:, k * 512:(k + 1) * 512], rb[:], ALU.mult)
            q.drain()
            ins = q.tensor_tensor(xn_s[:, k * 512:(k + 1) * 512], B["cvt"][:], mrb[:], ALU.subtract)
            q.drain()
        inc(ins, "vd")
        for pn, nt in PROJS:
            for j in range(nt):
                q.wait_ge(S["mm"].h, mm[("cpj", s, pn), j])
                bank = pb[4] if j % 2 == 0 else pb[6]
                if pn == "q":
                    dst = B["qpre"][:, j * QSL + PAD: j * QSL + PAD + 512]
                elif pn == "k":
                    dst = B["kpre"][:, j * QSL + PAD: j * QSL + PAD + 512]
                elif pn == "v":
                    dst = B["vpre"][:, j * QSL + PAD: j * QSL + PAD + 512]
                else:
                    dst = B["ggn"][:, j * 512:(j + 1) * 512]
                inc(q.tensor_copy(dst, bank[:]), "vd")
        q.wait_ge(S["mm"].h, mm["cba_mm", s])
        q.tensor_copy(B["beta_r"][:], pb[4][0:12, :])
        inc(q.tensor_copy(B["g_r"][:], pb[6][0:12, :]), "vd")
        wait(q, S["sc"], sc["cba", s])
        q.drain()
        q.tensor_tensor(B["g_r"][:], B["g_r"][:], B["cvt"][0:12, 0:512], ALU.add)
        q.drain()
        q.tensor_scalar(B["g_r"][:], B["g_r"][:], B["acol"][0:12, 0:1], None, ALU.mult)
        q.drain()
        for ct4 in range(4):
            q.tensor_tensor_scan(B["g_r"][:, ct4 * 128:(ct4 + 1) * 128],
                                 B["g_r"][:, ct4 * 128:(ct4 + 1) * 128],
                                 B["g_r"][:, ct4 * 128:(ct4 + 1) * 128],
                                 0.0, ALU.add, ALU.bypass)
            q.drain()
        inc(q.engine_nop(), "vd")
        for j in range(48):
            buf = (B["qpre"] if j < 12 else B["kpre"] if j < 24 else B["vpre"])
            jj = j if j < 12 else j - 12 if j < 24 else j - 24
            bc_ = jj * QSL + PAD
            w4 = B["convw"][:, j * 4:(j + 1) * 4]
            cvt = B["cvt"]
            if j >= 1:
                wait(q, S["sc"], sc[("ccv", s), j - 1])
                q.drain()
            q.tensor_scalar(cvt[:], buf[:, bc_ - 3: bc_ - 3 + 512], w4[:, 0:1], None, ALU.mult)
            q.drain()
            q.scalar_tensor_tensor(cvt[:], buf[:, bc_ - 2: bc_ - 2 + 512], w4[:, 1:2], cvt[:], ALU.mult, ALU.add)
            q.drain()
            q.scalar_tensor_tensor(cvt[:], buf[:, bc_ - 1: bc_ - 1 + 512], w4[:, 2:3], cvt[:], ALU.mult, ALU.add)
            q.drain()
            q.scalar_tensor_tensor(cvt[:], buf[:, bc_: bc_ + 512], w4[:, 3:4], cvt[:], ALU.mult, ALU.add)
            q.drain()
            ins = q.tensor_copy(buf[:, jj * QSL + 1: jj * QSL + PAD], buf[:, bc_ + 509: bc_ + 512])
            inc(ins, "vd")
        for i in range(24):
            buf = B["qpre"] if i < 12 else B["kpre"]
            jj = i if i < 12 else i - 12
            bc_ = jj * QSL + PAD
            q.wait_ge(S["mm"].h, mm[("cl2s", s), i])
            q.drain()
            inc(q.tensor_copy(B["l2r"][0:1, :], pb[7][0:1, 0:512]), "vd")
            wait(q, S["sc"], sc[("cl2r", s), i])
            q.drain()
            inc(q.tensor_copy(rows_bf[0:1, 0:512], B["l2r"][0:1, :]), "vd")
            q.wait_ge(S["mm"].h, mm[("cl2b", s), i])
            q.tensor_copy(rb[:], pb[6][:])
            q.drain()
            ins = q.tensor_tensor(buf[:, bc_:bc_ + 512], buf[:, bc_:bc_ + 512], rb[:], ALU.mult)
            inc(ins, "vd")
        for ct in range(4):
            q.wait_ge(S["mm"].h, mm[("cbg", s), ct])
            inc(q.tensor_copy(B["bgn"][:, ct * 24:(ct + 1) * 24], pb[6][:, 0:24]), "vd")
            for h in range(12):
                hp = h % 2
                bA = pb[0 + 2 * hp]
                bB = pb[1 + 2 * hp]
                r0 = 2 * hp * 128
                r1 = r0 + 128
                bgslot = ct * 24
                bcol_ap = B["bgn"][:, bgslot + h: bgslot + h + 1]
                gcol_ap = B["bgn"][:, bgslot + 12 + h: bgslot + 13 + h]
                if h >= 1:
                    q.wait_ge(S["mm"].h, mm[("cd3a", s, ct), h - 1])
                elif ct >= 1:
                    q.wait_ge(S["mm"].h, mm[("cd3a", s, ct - 1), 11])
                elif s >= 1:
                    q.wait_ge(S["mm"].h, mm[("cd3a", s - 1, 3), 11])
                q.drain()
                inc(q.tensor_copy(B["sb_t"][:], B["S_f"][:, h * 256:(h + 1) * 256]), "vd")
                q.wait_ge(S["mm"].h, mm[("cd1", s, ct), h])
                q.tensor_copy(B["grow"][0:1, :], bB[0:1, 384:512])
                q.tensor_copy(B["v_nat"][:, 0:128], pb[5][:, r0:r0 + 128])
                inc(q.tensor_copy(B["v_nat"][:, 128:256], pb[5][:, r1:r1 + 128]), "vd")
                q.wait_ge(S["mm"].h, mm[("cd1b", s, ct), h])
                q.drain()
                inc(q.tensor_scalar(B["nd_f"][:], bA[:, 128:256], gcol_ap, None, ALU.subtract), "vd")
                wait(q, S["sc"], sc[("cds1", s, ct), h])
                q.drain()
                q.tensor_tensor(B["m_f"][:], bA[:, 0:128], B["e_f"][:], ALU.mult)
                q.drain()
                q.tensor_tensor(B["m_f"][:], B["m_f"][:], B["mls"][:], ALU.mult)
                q.drain()
                q.tensor_scalar(B["a_bf"][:], B["m_f"][:], bcol_ap, -1.0, ALU.mult, ALU.mult)
                q.tensor_scalar(B["u_sc"][:], bB[:, 0:256], B["ccol"][:, 0:1], None, ALU.mult)
                q.drain()
                q.tensor_tensor(B["v_nat"][:], B["v_nat"][:], B["u_sc"][:], ALU.subtract)
                q.drain()
                q.tensor_scalar(B["b_bf"][:], B["v_nat"][:], bcol_ap, None, ALU.mult)
                q.drain()
                inc(q.tensor_copy(B["u_bf"][:], B["b_bf"][:]), "vd")
                for st, (kind, lf) in enumerate(LAD):
                    b0 = 64 * lf
                    q.wait_ge(S["mm"].h, mm[("cd2", s, ct, h), st])
                    q.drain()
                    if kind == "tra":
                        q.tensor_copy(B["at_bf"][:], pb[5][:, r0:r0 + 128])
                        q.drain()
                        for L2 in (0, 1):
                            c0 = 64 * L2
                            q.tensor_scalar(B["m_f"][c0:c0 + 64, 0:64],
                                            B["m_f"][c0:c0 + 64, c0:c0 + 64],
                                            B["bgn"][c0:c0 + 64, bgslot + h: bgslot + h + 1],
                                            -1.0, ALU.mult, ALU.mult)
                            q.drain()
                            ins = q.tensor_tensor(B["tt_f"][c0:c0 + 64, 0:64],
                                                  B["m_f"][c0:c0 + 64, 0:64],
                                                  B["id128f"][c0:c0 + 64, c0:c0 + 64], ALU.add)
                            q.drain()
                    elif kind == "p1t":
                        ins = q.tensor_copy(B["m_f"][b0:b0 + 64, 64:128], bA[b0:b0 + 64, 448:512])
                    elif kind == "sqp":
                        ins = q.tensor_copy(B["m_f"][b0:b0 + 64, 0:128], bA[b0:b0 + 64, 256:384])
                    elif kind == "sq1":
                        ins = q.tensor_copy(B["m_f"][b0:b0 + 64, 64:128], bA[b0:b0 + 64, 320:384])
                    elif kind == "prd":
                        ins = q.tensor_tensor(B["tt_f"][b0:b0 + 64, 0:64], B["tt_f"][b0:b0 + 64, 0:64],
                                              bA[b0:b0 + 64, 384:448], ALU.add)
                    elif kind == "tfin":
                        ins = q.tensor_copy(B["ttb"][b0:b0 + 64, 0:64], bA[b0:b0 + 64, 448:512])
                    elif kind == "apa":
                        ins = q.tensor_copy(B["u_bf"][0:64, :], bB[0:64, 0:256])
                    elif kind == "off":
                        ins = q.tensor_tensor(B["u_bf"][64:128, :], B["b_bf"][64:128, :],
                                              bB[64:128, 0:256], ALU.add)
                    else:
                        ins = q.tensor_copy(B["u_bf"][64:128, :], bB[64:128, 0:256])
                    inc(ins, "vd")
                q.wait_ge(S["mm"].h, mm[("cd3a", s, ct), h])
                q.tensor_scalar(B["o_f"][:], bB[:, 256:512], B["ccol"][:, 0:1], None, ALU.mult)
                q.tensor_copy(B["m_f"][:], bA[:, 0:128])
                q.drain()
                q.tensor_tensor(B["m_f"][:], B["m_f"][:], B["e_t"][:], ALU.mult)
                q.drain()
                inc(q.tensor_tensor(B["lx_bf"][:], B["m_f"][:], B["mui"][:], ALU.mult), "vd")
                q.wait_ge(S["mm"].h, mm[("cd3b", s, ct), h])
                q.drain()
                inc(q.tensor_tensor(B["o_f"][:], B["o_f"][:], bB[:, 0:256], ALU.add), "vd")
                q.wait_ge(S["mm"].h, mm[("cd4a", s, ct), h])
                q.tensor_copy(B["ccol"][:, 1:2], bA[:, 128:129])
                q.tensor_copy(B["k_nat"][:], pb[5][:, r0:r0 + 128])
                q.drain()
                inc(q.tensor_scalar(B["ccol"][:, 2:3], B["ccol"][:, 1:2], gcol_ap, None, ALU.subtract), "vd")
                wait(q, S["sc"], sc[("cds4", s, ct), h])
                q.drain()
                inc(q.tensor_scalar(B["b_bf"][:], B["u_bf"][:], B["ccol"][:, 2:3], None, ALU.mult), "vd")
                q.wait_ge(S["mm"].h, mm[("cd4b", s, ct), h])
                sslot = h * 256
                q.drain()
                ins = q.scalar_tensor_tensor(B["S_f"][:, sslot:sslot + 256], B["S_f"][:, sslot:sslot + 256],
                                             B["ccol"][:, 3:4], bB[:, 0:256], ALU.mult, ALU.add)
                inc(ins, "vd")
                wait(q, S["sc"], sc[("cosq", s, ct), h])
                q.tensor_reduce(B["ccol"][:, 4:5], B["u_sc"][:], AXX, ALU.add)
                q.drain()
                inc(q.tensor_scalar(B["ccol"][:, 4:5], B["ccol"][:, 4:5], 1.0 / 256, None, ALU.mult), "vd")
                wait(q, S["sc"], sc[("cds5", s, ct), h])
                q.drain()
                inc(q.tensor_scalar(B["onb"][:], B["o_f"][:], B["ccol"][:, 4:5], None, ALU.mult), "vd")
                q.wait_ge(S["mm"].h, mm[("cd5", s, ct), h])
                for j in range(2):
                    q.tensor_scalar(B["u_sc"][:, 0:128], pb[5][:, (r0 if j == 0 else r1):(r0 if j == 0 else r1) + 128],
                                    B["nw_col"][:, j:j + 1], None, ALU.mult)
                    q.drain()
                    slot = (2 * h + j) * 512 + ct * 128
                    ins = q.tensor_tensor(B["oT"][:, slot:slot + 128], B["u_sc"][:, 0:128],
                                          B["sgg"][:, j * 128:(j + 1) * 128], ALU.mult)
                    q.drain()
                inc(ins, "vd")
        for j in range(16):
            q.wait_ge(S["mm"].h, mm[("cwo", s, j // 4), j % 4])
            bank = pb[4] if j % 2 == 0 else pb[6]
            inc(q.tensor_copy(xn_s[:, j * 512:(j + 1) * 512], bank[:]), "vd")
        q.wait_ge(S["mm"].h, mm[("cwsq", s), 15])
        q.tensor_scalar(rows[:, 0:512], pb[7][0:1, :], 1.0 / 2048, None, ALU.mult)
        q.tensor_scalar(rows[:, 512:1024], pb[0][0:1, :], 1.0 / 2048, None, ALU.mult)
        q.drain()
        q.tensor_tensor(B["l2r"][0:1, :], rows[:, 0:512], rows[:, 0:512], ALU.mult)
        q.drain()
        inc(q.tensor_tensor(rows[:, 512:1024], rows[:, 512:1024], B["l2r"][0:1, :], ALU.subtract), "vd")
        wait(q, S["sc"], sc["cwr", s])
        q.drain()
        q.tensor_copy(rows_bf[:, 0:512], rows[:, 512:1024])
        q.tensor_tensor(B["l2r"][0:1, :], rows[:, 0:512], rows[:, 512:1024], ALU.mult)
        q.drain()
        inc(q.tensor_copy(rows_bf[:, 512:1024], B["l2r"][0:1, :]), "vd")
        q.wait_ge(S["mm"].h, mm["cwbc", s])
        q.tensor_copy(rb[:], pb[7][:])
        q.tensor_copy(mrb[:], pb[0][:])
        q.drain()
        for k in range(16):
            gi = s * 16 + k
            q.wait_ge(S["lf%d" % (k % 2)].h, 16 * (gi // 2 + 1))
            if gi >= 1:
                q.wait_ge(S["sty"].h, 16 * gi)
            ff = B["ffin"][k % 2]
            q.tensor_tensor(ff[:], xn_s[:, k * 512:(k + 1) * 512], rb[:], ALU.mult)
            q.drain()
            q.tensor_tensor(ff[:], ff[:], mrb[:], ALU.subtract)
            q.drain()
            ins = q.tensor_tensor(ff[:], ff[:], B["x_fin"][k % 2][:], ALU.add)
            inc(ins, "vd")

    @blk.tensor
    def _(q):
        xn_s = B["xn_s"]; on_c = B["on_c"]; on_r = B["on_r"]
        rows_bf = B["rows_bf"]
        wbuf = B["wbuf"]
        if s >= 1:
            wait(q, S["vd"], vd[("cfin", s - 1), 15])
        q.wait_ge(S["lx"].h, lx_base + 256)
        for k in range(16):
            q.matmul(pb[6][0:1, :], on_c[:], xn_s[:, k * 512:(k + 1) * 512],
                     start=(k == 0), stop=(k == 15))
            wait(q, S["sc"], sc[("cq", s), k])
            inc(q.matmul(pb[7][0:1, :], on_c[:], B["xsq"][k % 2][:],
                         start=(k == 0), stop=(k == 15)), "mm")
        wait(q, S["vd"], vd["cbf1", s])
        q.matmul(pb[6][:], on_r[:], rows_bf[0:1, 0:512], start=True, stop=True)
        inc(q.matmul(pb[7][:], on_r[:], rows_bf[0:1, 512:1024], start=True, stop=True), "mm")
        wait(q, S["vd"], vd["cxn", s])
        lin = 0
        prev_pn = None
        prev_nt = 0
        for pn, nt in PROJS:
            for oc in range(nt // 4):
                b, lwv = loadrec[("wchunk", s, pn, oc)]
                q.wait_ge(S["lw%d" % b].h, lwv)
                for otl in range(4):
                    j = oc * 4 + otl
                    if j >= 2:
                        wait(q, S["vd"], vd[("cpe", s, pn), j - 2])
                    elif prev_pn is not None:
                        wait(q, S["vd"], vd[("cpe", s, prev_pn), prev_nt - 2 + j])
                    bank = pb[4] if j % 2 == 0 else pb[6]
                    for k in range(16):
                        ins = q.matmul(bank[:],
                                       wbuf[b][:, k * 512 + otl * 128: k * 512 + (otl + 1) * 128],
                                       xn_s[:, k * 512:(k + 1) * 512],
                                       start=(k == 0), stop=(k == 15))
                    inc(ins, "mm")
                lin += 1
            prev_pn, prev_nt = pn, nt
        b, lwv = loadrec[("ba", s)]
        q.wait_ge(S["lw%d" % b].h, lwv)
        wait(q, S["vd"], vd[("cpe", s, "g"), 23])
        for k in range(16):
            q.matmul(pb[4][0:12, :], wbuf[b][:, k * 24:(k + 1) * 24][:, 0:12],
                     xn_s[:, k * 512:(k + 1) * 512], start=(k == 0), stop=(k == 15))
        for k in range(16):
            ins = q.matmul(pb[6][0:12, :], wbuf[b][:, 384 + k * 24: 384 + k * 24 + 12],
                           xn_s[:, k * 512:(k + 1) * 512], start=(k == 0), stop=(k == 15))
        inc(ins, "mm")
        for i in range(24):
            wait(q, S["sc"], sc[("cl2q", s), i])
            inc(q.matmul(pb[7][0:1, 0:512], on_c[:], B["xsq"][0][:], start=True, stop=True), "mm")
            wait(q, S["vd"], vd[("cl2c", s), i])
            inc(q.matmul(pb[6][:], on_r[:], rows_bf[0:1, 0:512], start=True, stop=True), "mm")
        for ct in range(4):
            if ct >= 1:
                wait(q, S["vd"], vd[("cbg_ev", s), ct - 1])
            else:
                wait(q, S["vd"], vd[("cl2m", s), 23])
            wait(q, S["vd"], vd["cbae", s])
            q.transpose(pb[6][:, 0:12], B["beta_r"][:, ct * 128:(ct + 1) * 128], B["id128f"][0:12, 0:12])
            inc(q.transpose(pb[6][:, 12:24], B["g_r"][:, ct * 128:(ct + 1) * 128], B["id128f"][0:12, 0:12]), "mm")
            for h in range(12):
                hp = h % 2
                bA = pb[0 + 2 * hp]
                bB = pb[1 + 2 * hp]
                r0 = 2 * hp * 128
                r1 = r0 + 128
                qsl = h * QSL + PAD + ct * 128
                vsl0 = (2 * h) * QSL + PAD + ct * 128
                vsl1 = (2 * h + 1) * QSL + PAD + ct * 128
                KT = B["kpre"][:, qsl:qsl + 128]
                QT = B["qpre"][:, qsl:qsl + 128]
                bgslot = ct * 24
                if h >= 2:
                    wait(q, S["vd"], vd[("cdv5", s, ct), h - 2])
                elif ct >= 1:
                    wait(q, S["vd"], vd[("cdv5", s, ct - 1), h + 10])
                elif s >= 1:
                    wait(q, S["vd"], vd[("cdv5", s - 1, 3), h + 10])
                wait(q, S["vd"], vd[("cbg_ev", s), ct])
                if h < 2:
                    wait(q, S["vd"], vd[("cl2m", s), 23])
                q.transpose(bB[0:1, 384:512], B["bgn"][:, bgslot + 12 + h: bgslot + 13 + h], B["id128f"][:])
                q.matmul(bA[:, 0:128], KT, KT, start=True, stop=True)
                wait(q, S["vd"], vd[("cdsb", s, ct), h])
                q.matmul(bB[:, 0:256], KT, B["sb_t"][:], start=True, stop=True)
                q.transpose(pb[5][:, r0:r0 + 128], B["vpre"][:, vsl0:vsl0 + 128], B["id128"][:])
                inc(q.transpose(pb[5][:, r1:r1 + 128], B["vpre"][:, vsl1:vsl1 + 128], B["id128"][:]), "mm")
                wait(q, S["vd"], vd[("cdv1a", s, ct), h])
                inc(q.matmul(bA[:, 128:256], B["on_rf"][:], B["grow"][0:1, :], start=True, stop=True), "mm")
                for st, (kind, lf) in enumerate(LAD):
                    b0 = 64 * lf
                    if st == 0:
                        wait(q, S["vd"], vd[("cdv1", s, ct), h])
                    else:
                        wait(q, S["vd"], vd[("cdl", s, ct, h), st - 1])
                    if kind == "tra":
                        ins = q.transpose(pb[5][:, r0:r0 + 128], B["a_bf"][:], B["id128"][:])
                    elif kind == "p1t":
                        ins = q.matmul(bA[b0:b0 + 64, 448:512], B["m_f"][b0:b0 + 64, 0:64],
                                       B["id128f"][b0:b0 + 64, b0:b0 + 64], start=True, stop=True)
                    elif kind == "sqp":
                        q.matmul(bA[b0:b0 + 64, 256:320], B["m_f"][b0:b0 + 64, 64:128],
                                 B["m_f"][b0:b0 + 64, 0:64], start=True, stop=True)
                        ins = q.matmul(bA[b0:b0 + 64, 320:384], B["m_f"][b0:b0 + 64, 0:64],
                                       B["m_f"][b0:b0 + 64, 64:128], start=True, stop=True)
                    elif kind == "sq1":
                        ins = q.matmul(bA[b0:b0 + 64, 320:384], B["m_f"][b0:b0 + 64, 0:64],
                                       B["m_f"][b0:b0 + 64, 64:128], start=True, stop=True)
                    elif kind == "prd":
                        ins = q.matmul(bA[b0:b0 + 64, 384:448], B["m_f"][b0:b0 + 64, 64:128],
                                       B["tt_f"][b0:b0 + 64, 0:64], start=True, stop=True)
                    elif kind == "tfin":
                        ins = q.matmul(bA[b0:b0 + 64, 448:512], B["tt_f"][b0:b0 + 64, 0:64],
                                       B["id128f"][b0:b0 + 64, b0:b0 + 64], start=True, stop=True)
                    elif kind == "apa":
                        ins = q.matmul(bB[0:64, 0:256], B["ttb"][0:64, 0:64], B["u_bf"][0:64, :],
                                       start=True, stop=True)
                    elif kind == "off":
                        ins = q.matmul(bB[64:128, 0:256], B["at_bf"][0:64, 64:128], B["u_bf"][0:64, :],
                                       start=True, stop=True)
                    else:
                        ins = q.matmul(bB[64:128, 0:256], B["ttb"][64:128, 0:64], B["u_bf"][64:128, :],
                                       start=True, stop=True)
                    inc(ins, "mm")
                wait(q, S["vd"], vd[("cdl", s, ct, h), len(LAD) - 1])
                q.matmul(bB[:, 256:512], QT, B["sb_t"][:], start=True, stop=True)
                inc(q.matmul(bA[:, 0:128], KT, QT, start=True, stop=True), "mm")
                wait(q, S["vd"], vd[("cdv3a", s, ct), h])
                inc(q.matmul(bB[:, 0:256], B["lx_bf"][:], B["u_bf"][:], start=True, stop=True), "mm")
                wait(q, S["vd"], vd[("cdv3", s, ct), h])
                q.matmul(bA[:, 128:129], B["on_rf"][:], B["grow"][0:1, 127:128], start=True, stop=True)
                inc(q.transpose(pb[5][:, r0:r0 + 128], KT, B["id128"][:]), "mm")
                wait(q, S["vd"], vd[("cdv4u", s, ct), h])
                inc(q.matmul(bB[:, 0:256], B["k_nat"][:], B["b_bf"][:], start=True, stop=True), "mm")
                wait(q, S["vd"], vd[("cdv5b", s, ct), h])
                q.transpose(pb[5][:, r0:r0 + 128], B["onb"][:, 0:128], B["id128"][:])
                inc(q.transpose(pb[5][:, r1:r1 + 128], B["onb"][:, 128:256], B["id128"][:]), "mm")
        for j in range(16):
            oc, otl = j // 4, j % 4
            b, lwv = loadrec[("wo", s, oc)]
            q.wait_ge(S["lw%d" % b].h, lwv)
            if j == 0:
                wait(q, S["vd"], vd[("cdv5", s, 3), 11])
            if j >= 2:
                wait(q, S["vd"], vd[("cwe", s), j - 2])
            bank = pb[4] if j % 2 == 0 else pb[6]
            for k in range(24):
                ins = q.matmul(bank[:],
                               wbuf[b][:, k * 512 + otl * 128: k * 512 + (otl + 1) * 128],
                               B["oT"][:, k * 512:(k + 1) * 512],
                               start=(k == 0), stop=(k == 23))
            inc(ins, "mm")
            wait(q, S["vd"], vd[("cwe", s), j])
            q.matmul(pb[7][0:1, :], on_c[:], xn_s[:, j * 512:(j + 1) * 512],
                     start=(j == 0), stop=(j == 15))
            wait(q, S["sc"], sc[("cwq", s), j])
            inc(q.matmul(pb[0][0:1, :], on_c[:], B["xsq"][j % 2][:],
                         start=(j == 0), stop=(j == 15)), "mm")
        wait(q, S["vd"], vd["cwbf", s])
        q.matmul(pb[7][:], on_r[:], rows_bf[0:1, 0:512], start=True, stop=True)
        inc(q.matmul(pb[0][:], on_r[:], rows_bf[0:1, 512:1024], start=True, stop=True), "mm")


def build_phase_c():
    nc = bass.Bass("TRN2", target_bir_lowering=False)
    xT = nc.dram_tensor("xT", [D, TPC], F32, kind="ExternalInput").ap()
    Wq = nc.dram_tensor("L1_attn_Wq", [2048, 1536], F32, kind="ExternalInput").ap()
    Wk = nc.dram_tensor("L1_attn_Wk", [2048, 1536], F32, kind="ExternalInput").ap()
    Wv = nc.dram_tensor("L1_attn_Wv", [2048, 3072], F32, kind="ExternalInput").ap()
    Wb = nc.dram_tensor("L1_attn_Wb", [2048, 12], F32, kind="ExternalInput").ap()
    Wa = nc.dram_tensor("L1_attn_Wa", [2048, 12], F32, kind="ExternalInput").ap()
    Wg = nc.dram_tensor("L1_attn_Wg", [2048, 3072], F32, kind="ExternalInput").ap()
    Wo = nc.dram_tensor("L1_attn_Wo", [3072, 2048], F32, kind="ExternalInput").ap()
    cwq = nc.dram_tensor("L1_attn_conv_wq", [1536, 4], F32, kind="ExternalInput").ap()
    cwk = nc.dram_tensor("L1_attn_conv_wk", [1536, 4], F32, kind="ExternalInput").ap()
    cwv = nc.dram_tensor("L1_attn_conv_wv", [3072, 4], F32, kind="ExternalInput").ap()
    A_log = nc.dram_tensor("L1_attn_A_log", [12], F32, kind="ExternalInput").ap()
    dt_bias = nc.dram_tensor("L1_attn_dt_bias", [12], F32, kind="ExternalInput").ap()
    norm_w = nc.dram_tensor("L1_attn_norm_w", [256], F32, kind="ExternalInput").ap()
    yT = nc.dram_tensor("yT", [D, TPC], F32, kind="ExternalOutput").ap()
    C = declare_consts(nc, ["c_ones_col", "c_ones_row", "c_ones_rowf", "c_misc",
                            "c_id128", "c_id128f", "c_mask_ls", "c_mask_ui"])

    es = ExitStack()
    sb = lambda n, sh, dt: es.enter_context(nc.sbuf_tensor(n, sh, dt))
    psb = lambda n, sh, dt: es.enter_context(nc.psum_tensor(n, sh, dt))

    PAD = 4
    QSL = 512 + PAD
    xn_s = sb("xn_s", [128, 16 * 512], BF16)
    qpre = sb("qpre", [128, 12 * QSL], BF16)
    kpre = sb("kpre", [128, 12 * QSL], BF16)
    vpre = sb("vpre", [128, 24 * QSL], BF16)
    ggn = sb("ggn", [128, 4 * 3072], BF16)
    oT = sb("oT", [128, 24 * 512], BF16)
    wbuf = [sb("wbuf0", [128, 24 * 512], BF16), sb("wbuf1", [128, 16 * 512], BF16)]
    x_fin = [sb(f"x_fin{i}", [128, 512], F32) for i in range(2)]
    ffin = [sb(f"ffin{i}", [128, 512], F32) for i in range(2)]
    xsq = [sb(f"xsq{i}", [128, 512], BF16) for i in range(2)]
    cvt = sb("cvt", [128, 512], F32)
    S_f = sb("S_f", [128, 12 * 256], F32)
    sb_t = sb("sb_t", [128, 256], BF16)
    convw = sb("convw", [128, 48 * 4], F32)
    beta_r = sb("beta_r", [12, 512], F32)
    g_r = sb("g_r", [12, 512], F32)
    gcar = sb("gcar", [12, 1], F32)
    acol = sb("acol", [12, 2], F32)
    bgn = sb("bgn", [128, 4 * 24], F32)
    grow = sb("grow", [1, 128], F32)
    nd_f = sb("nd_f", [128, 128], F32)
    e_f = sb("e_f", [128, 128], F32)
    e_t = sb("e_t", [128, 128], F32)
    m_f = sb("m_f", [128, 128], F32)
    a_bf = sb("a_bf", [128, 128], BF16)
    at_bf = sb("at_bf", [128, 128], BF16)
    tt_f = sb("tt_f", [128, 64], F32)
    ttb = sb("ttb", [128, 64], BF16)
    lx_bf = sb("lx_bf", [128, 128], BF16)
    v_nat = sb("v_nat", [128, 256], F32)
    k_nat = sb("k_nat", [128, 128], BF16)
    b_bf = sb("b_bf", [128, 256], BF16)
    u_bf = sb("u_bf", [128, 256], BF16)
    u_sc = sb("u_sc", [128, 256], F32)
    o_f = sb("o_f", [128, 256], F32)
    sgg = sb("sgg", [128, 256], F32)
    onb = sb("onb", [128, 256], BF16)
    ccol = sb("ccol", [128, 8], F32)
    l2r = sb("l2r", [1, 512], F32)
    rows = sb("rows", [1, 2 * 512], F32)
    rows_bf = sb("rows_bf", [1, 2 * 512], BF16)
    rb = sb("rb", [128, 512], F32)
    mrb = sb("mrb", [128, 512], F32)
    on_c = sb("on_c", [128, 1], BF16)
    on_r = sb("on_r", [1, 128], BF16)
    on_rf = sb("on_rf", [1, 128], F32)
    id128 = sb("id128", [128, 128], BF16)
    id128f = sb("id128f", [128, 128], F32)
    mls = sb("mls", [128, 128], F32)
    mui = sb("mui", [128, 128], F32)
    misc_sb = sb("misc_sb", [128, 8], F32)
    nw_col = sb("nw_col", [128, 2], F32)

    pb = [psb(f"pb{i}", [128, 512], F32 if i != 5 else BF16) for i in range(8)]

    S = {k: Sem(es.enter_context(nc.semaphore("sem_" + k))) for k in
         ["ld", "lx", "lw0", "lw1", "lf0", "lf1", "sty", "sc", "vd", "mm", "aux"]}
    N = {k: 0 for k in S}
    loadrec = {}

    def dma(q, key, out, in_):
        N[key] += 16
        return q.dma_start(out=out, in_=in_).then_inc(S[key].h, 16)

    mm, sc, vd = Sched(), Sched(), Sched()
    eps5 = misc_sb[0:1, 0:1]
    eps6 = misc_sb[0:1, 1:2]
    eps5c = misc_sb[:, 0:1]

    def inc(ins, key):
        return ins.then_inc(S[key].h, 1)

    PROJS = [("q", 12), ("k", 12), ("v", 24), ("g", 24)]
    LVSEQ = [1, 2, 4, 8, 16, 32, 64]

    def reg_slab(s):
        for k in range(16):
            sc.add(("cq", s), k)
        sc.add("crs1", s)
        sc.add("cba", s)
        for j in range(48):
            sc.add(("ccv", s), j)
        for i in range(24):
            sc.add(("cl2q", s), i)
            sc.add(("cl2r", s), i)
        for ct in range(4):
            for h in range(12):
                sc.add(("cds1", s, ct), h)
                sc.add(("cds4", s, ct), h)
                sc.add(("cosq", s, ct), h)
                sc.add(("cds5", s, ct), h)
        for k in range(16):
            sc.add(("cwq", s), k)
        sc.add("cwr", s)
        vd.add("cr1", s)
        vd.add("cbf1", s)
        vd.add("cxn", s)
        for pn, nt in PROJS:
            for j in range(nt):
                vd.add(("cpe", s, pn), j)
        vd.add("cbaeA", s)
        vd.add("cbae", s)
        for j in range(48):
            vd.add(("ccva", s), j)
        for i in range(24):
            vd.add(("cl2v", s), i)
            vd.add(("cl2c", s), i)
            vd.add(("cl2m", s), i)
        for ct in range(4):
            vd.add(("cbg_ev", s), ct)
            for h in range(12):
                vd.add(("cdsb", s, ct), h)
                vd.add(("cdv1a", s, ct), h)
                vd.add(("cdv1b", s, ct), h)
                vd.add(("cdv1", s, ct), h)
                for st in range(28):
                    vd.add(("cdl", s, ct, h), st)
                vd.add(("cdv3a", s, ct), h)
                vd.add(("cdv3", s, ct), h)
                vd.add(("cdv4a", s, ct), h)
                vd.add(("cdv4u", s, ct), h)
                vd.add(("cdv4", s, ct), h)
                vd.add(("cdv5a", s, ct), h)
                vd.add(("cdv5b", s, ct), h)
                vd.add(("cdv5", s, ct), h)
        for k in range(16):
            vd.add(("cwe", s), k)
        vd.add("cwr1", s)
        vd.add("cwbf", s)
        for k in range(16):
            vd.add(("cfin", s), k)
        for k in range(16):
            mm.add(("csq", s), k)
        mm.add("cbc1", s)
        for pn, nt in PROJS:
            for j in range(nt):
                mm.add(("cpj", s, pn), j)
        mm.add("cba_mm", s)
        for i in range(24):
            mm.add(("cl2s", s), i)
            mm.add(("cl2b", s), i)
        for ct in range(4):
            mm.add(("cbg", s), ct)
            for h in range(12):
                mm.add(("cd1", s, ct), h)
                mm.add(("cd1b", s, ct), h)
                for st in range(28):
                    mm.add(("cd2", s, ct, h), st)
                mm.add(("cd3a", s, ct), h)
                mm.add(("cd3b", s, ct), h)
                mm.add(("cd4a", s, ct), h)
                mm.add(("cd4b", s, ct), h)
                mm.add(("cd5", s, ct), h)
        for j in range(16):
            mm.add(("cwo", s, j // 4), j % 4)
            mm.add(("cwsq", s), j)
        mm.add("cwbc", s)

    with es:
        with nc.Block() as blk:
            @blk.sync
            def _(q):
                for ap_, dst in [(C["c_ones_col"], on_c[:]), (C["c_ones_row"], on_r[:]),
                                 (C["c_ones_rowf"], on_rf[:]),
                                 (C["c_misc"], misc_sb[:]), (C["c_id128"], id128[:]),
                                 (C["c_id128f"], id128f[:]),
                                 (C["c_mask_ls"], mls[:]), (C["c_mask_ui"], mui[:])]:
                    dma(q, "ld", dst, ap_)
                for j in range(12):
                    dma(q, "ld", convw[:, j * 4:(j + 1) * 4], cwq[j * 128:(j + 1) * 128, :])
                for j in range(12):
                    dma(q, "ld", convw[:, (12 + j) * 4:(13 + j) * 4], cwk[j * 128:(j + 1) * 128, :])
                for j in range(24):
                    dma(q, "ld", convw[:, (24 + j) * 4:(25 + j) * 4], cwv[j * 128:(j + 1) * 128, :])
                dma(q, "ld", acol[0:12, 0:1], A_log.rearrange("(p o) -> p o", o=1))
                dma(q, "ld", acol[0:12, 1:2], dt_bias.rearrange("(p o) -> p o", o=1))

            @blk.gpsimd
            def _(q):
                dma(q, "lx", nw_col[:, 0:1], norm_w[0:128].rearrange("(p o) -> p o", o=1))
                dma(q, "lx", nw_col[:, 1:2], norm_w[128:256].rearrange("(p o) -> p o", o=1))

            @blk.scalar
            def _(q):
                q.wait_ge(S["ld"].h, N["ld"])
                q.wait_ge(S["lx"].h, N["lx"])
                inc(q.activation(acol[0:12, 0:1], acol[0:12, 0:1], AF.Exp), "sc")
        sc.add("negA0")
        with nc.Block() as blk:
            @blk.vector
            def _(q):
                wait(q, S["sc"], sc["negA0"])
                q.drain()
                q.tensor_scalar(acol[0:12, 0:1], acol[0:12, 0:1], -1.0, None, ALU.mult)
                q.memset(gcar[:], 0.0)
                q.memset(qpre[:], 0.0)
                q.memset(kpre[:], 0.0)
                q.memset(vpre[:], 0.0)
                q.memset(S_f[:], 0.0)
                q.memset(sb_t[:], 0.0)
                inc(q.drain(), "vd")
        vd.add("pre")

        for bi in range(2):
            for sl in range(4):
                s = bi * 4 + sl
                reg_slab(s)
                lx_base = N["lx"]
                ld_base = N["ld"]
                tok0 = bi * 2048 + sl * 512

                with nc.Block() as blk:
                    @blk.gpsimd
                    def _(q, s=s, sl=sl, bi=bi, tok0=tok0):
                        if s >= 1:
                            q.wait_ge(S["mm"].h, mm["cba_mm", s - 1])
                        for k in range(16):
                            dma(q, "lx", xn_s[:, k * 512:(k + 1) * 512],
                                xT[k * 128:(k + 1) * 128, tok0:tok0 + 512])
                        lin = 0
                        chunk_list = []
                        for pn, nt in PROJS:
                            for oc in range(nt // 4):
                                chunk_list.append((pn, oc))
                        for pn, oc in chunk_list:
                            b = lin % 2
                            gl = s * 13 + lin
                            if gl >= 2:
                                key = ("chunkdone", gl - 2)
                                q.wait_ge(S["mm"].h, loadrec[key])
                            W = {"q": Wq, "k": Wk, "v": Wv, "g": Wg}[pn]
                            for k in range(16):
                                dma(q, "lw%d" % b, wbuf[b][:, k * 512:(k + 1) * 512],
                                    W[k * 128:(k + 1) * 128, oc * 512:(oc + 1) * 512])
                            loadrec[("wchunk", s, pn, oc)] = (b, N["lw%d" % b])
                            loadrec[("chunkdone", s * 13 + lin)] = mm[("cpj", s, pn), oc * 4 + 3]
                            lin += 1
                        b = lin % 2
                        gl = s * 13 + lin
                        if gl >= 2:
                            q.wait_ge(S["mm"].h, loadrec[("chunkdone", gl - 2)])
                        for k in range(16):
                            dma(q, "lw%d" % b, wbuf[b][:, k * 24: k * 24 + 12],
                                Wb[k * 128:(k + 1) * 128, :])
                        for k in range(16):
                            dma(q, "lw%d" % b, wbuf[b][:, 384 + k * 24: 384 + k * 24 + 12],
                                Wa[k * 128:(k + 1) * 128, :])
                        loadrec[("ba", s)] = (b, N["lw%d" % b])
                        loadrec[("chunkdone", s * 13 + 12)] = mm["cba_mm", s]
                        for oc in range(4):
                            b = 0
                            gl2 = s * 4 + oc
                            if oc == 0:
                                q.wait_ge(S["mm"].h, mm["cba_mm", s])
                            if gl2 >= 1:
                                s2, oc2 = divmod(gl2 - 1, 4)
                                q.wait_ge(S["mm"].h, mm[("cwo", s2, oc2), 3])
                            for k in range(24):
                                dma(q, "lw%d" % b, wbuf[b][:, k * 512:(k + 1) * 512],
                                    Wo[k * 128:(k + 1) * 128, oc * 512:(oc + 1) * 512])
                            loadrec[("wo", s, oc)] = (b, N["lw%d" % b])
                        for k in range(16):
                            wait(q, S["vd"], vd[("cfin", s), k])
                            dma(q, "sty", yT[k * 128:(k + 1) * 128, tok0:tok0 + 512],
                                ffin[k % 2][:])

                    @blk.sync
                    def _(q, s=s, tok0=tok0):
                        for k in range(16):
                            gi = s * 16 + k
                            if gi >= 2:
                                p = gi - 2
                                wait(q, S["vd"], vd[("cfin", p // 16), p % 16])
                            dma(q, "lf%d" % (k % 2), x_fin[k % 2][:],
                                xT[k * 128:(k + 1) * 128, tok0:tok0 + 512])

                    @blk.scalar
                    def _(q, s=s, lx_base=lx_base):
                        q.wait_ge(S["lx"].h, lx_base + 256)
                        for k in range(16):
                            if k >= 2:
                                q.wait_ge(S["mm"].h, mm[("csq", s), k - 2])
                            elif s >= 1:
                                q.wait_ge(S["mm"].h, mm[("csq", s - 1), 14 + k])
                            inc(q.activation(xsq[k % 2][:], xn_s[:, k * 512:(k + 1) * 512], AF.Square), "sc")
                        wait(q, S["vd"], vd["cr1", s])
                        q.activation(rows[:, 512:1024], rows[:, 512:1024], AF.Ln, bias=eps5)
                        q.drain()
                        inc(q.activation(rows[:, 512:1024], rows[:, 512:1024], AF.Exp, scale=-0.5), "sc")
                        wait(q, S["vd"], vd["cbaeA", s])
                        q.activation(beta_r[:], beta_r[:], AF.Sigmoid)
                        q.drain()
                        q.activation(cvt[0:12, 0:512], g_r[:], AF.Abs, bias=acol[0:12, 1:2])
                        q.drain()
                        q.activation(cvt[0:12, 0:512], cvt[0:12, 0:512], AF.Exp, scale=-1.0)
                        q.drain()
                        q.activation(cvt[0:12, 0:512], cvt[0:12, 0:512], AF.Ln, bias=1.0)
                        q.drain()
                        inc(q.activation(g_r[:], g_r[:], AF.Relu, bias=acol[0:12, 1:2]), "sc")
                        for j in range(48):
                            wait(q, S["vd"], vd[("ccva", s), j])
                            dst = (qpre if j < 12 else kpre if j < 24 else vpre)
                            jj = j if j < 12 else j - 12 if j < 24 else j - 24
                            inc(q.activation(dst[:, jj * QSL + PAD: jj * QSL + PAD + 512], cvt[:], AF.Silu), "sc")
                        for i in range(24):
                            buf = qpre if i < 12 else kpre
                            jj = i if i < 12 else i - 12
                            base_c = jj * QSL + PAD
                            wait(q, S["vd"], vd[("cl2m", s), i - 1] if i >= 1 else vd[("ccva", s), 47])
                            wait(q, S["sc"], sc[("ccv", s), i])
                            inc(q.activation(xsq[0][:], buf[:, base_c:base_c + 512], AF.Square), "sc")
                            wait(q, S["vd"], vd[("cl2v", s), i])
                            q.activation(l2r[0:1, :], l2r[0:1, :], AF.Ln, bias=eps6)
                            q.drain()
                            inc(q.activation(l2r[0:1, :], l2r[0:1, :], AF.Exp, scale=-0.5), "sc")
                        for ct in range(4):
                            for h in range(12):
                                bgslot = ct * 24
                                wait(q, S["vd"], vd[("cdv1b", s, ct), h])
                                q.activation(e_f[:], nd_f[:], AF.Relu)
                                q.drain()
                                q.activation(e_f[:], e_f[:], AF.Exp, scale=-1.0)
                                q.activation(e_t[:], nd_f[:], AF.Relu, scale=-1.0)
                                q.drain()
                                q.activation(e_t[:], e_t[:], AF.Exp, scale=-1.0)
                                inc(q.activation(ccol[:, 0:1], bgn[:, bgslot + 12 + h: bgslot + 13 + h], AF.Exp), "sc")
                                wait(q, S["vd"], vd[("cdv4a", s, ct), h])
                                q.activation(ccol[:, 2:3], ccol[:, 2:3], AF.Exp)
                                inc(q.activation(ccol[:, 3:4], ccol[:, 1:2], AF.Exp), "sc")
                                wait(q, S["vd"], vd[("cdv3", s, ct), h])
                                inc(q.activation(u_sc[:], o_f[:], AF.Square), "sc")
                                wait(q, S["vd"], vd[("cdv5a", s, ct), h])
                                q.activation(ccol[:, 4:5], ccol[:, 4:5], AF.Ln, bias=eps5c)
                                q.drain()
                                q.activation(ccol[:, 4:5], ccol[:, 4:5], AF.Exp, scale=-0.5)
                                g0 = (2 * h) * 512 + ct * 128
                                g1 = (2 * h + 1) * 512 + ct * 128
                                q.activation(sgg[:, 0:128], ggn[:, g0:g0 + 128], AF.Silu)
                                inc(q.activation(sgg[:, 128:256], ggn[:, g1:g1 + 128], AF.Silu), "sc")
                        for k in range(16):
                            wait(q, S["vd"], vd[("cwe", s), k])
                            if k >= 2:
                                q.wait_ge(S["mm"].h, mm[("cwsq", s), k - 2])
                            inc(q.activation(xsq[k % 2][:], xn_s[:, k * 512:(k + 1) * 512], AF.Square), "sc")
                        wait(q, S["vd"], vd["cwr1", s])
                        q.activation(rows[:, 512:1024], rows[:, 512:1024], AF.Ln, bias=eps5)
                        q.drain()
                        inc(q.activation(rows[:, 512:1024], rows[:, 512:1024], AF.Exp, scale=-0.5), "sc")
                    _PHASE_C_CONT(nc, blk, q_args=dict(
                        s=s, bi=bi, sl=sl, tok0=tok0, lx_base=lx_base, ld_base=ld_base,
                        S=S, N=N, mm=mm, sc=sc, vd=vd, loadrec=loadrec,
                        bufs=dict(xn_s=xn_s, qpre=qpre, kpre=kpre, vpre=vpre,
                                  ggn=ggn, oT=oT, wbuf=wbuf, x_fin=x_fin, ffin=ffin,
                                  xsq=xsq, cvt=cvt, S_f=S_f, sb_t=sb_t,
                                  convw=convw, beta_r=beta_r, g_r=g_r,
                                  gcar=gcar, acol=acol, bgn=bgn, grow=grow, nw_col=nw_col,
                                  nd_f=nd_f, e_f=e_f, e_t=e_t, m_f=m_f, a_bf=a_bf,
                                  at_bf=at_bf, tt_f=tt_f, ttb=ttb, lx_bf=lx_bf,
                                  v_nat=v_nat, k_nat=k_nat, b_bf=b_bf, u_bf=u_bf,
                                  u_sc=u_sc, o_f=o_f, sgg=sgg, onb=onb,
                                  ccol=ccol, l2r=l2r, rows=rows,
                                  rows_bf=rows_bf, rb=rb, mrb=mrb, on_c=on_c,
                                  on_r=on_r, id128=id128, mls=mls, mui=mui,
                                  misc_sb=misc_sb, pb=pb, id128f=id128f,
                                  on_rf=on_rf)))

        with nc.Block() as blk:
            @blk.gpsimd
            def _(q):
                q.wait_ge(S["sty"].h, N["sty"])
    return nc



def build_phase_a():
    nc = bass.Bass("TRN2", target_bir_lowering=False)
    xN = nc.dram_tensor("x", [2, SEQ, D], F32, kind="ExternalInput").ap()
    Wq = nc.dram_tensor("L0_attn_Wq", [D, D], F32, kind="ExternalInput").ap()
    Wk = nc.dram_tensor("L0_attn_Wk", [D, D], F32, kind="ExternalInput").ap()
    Wv = nc.dram_tensor("L0_attn_Wv", [D, D], F32, kind="ExternalInput").ap()
    Wo = nc.dram_tensor("L0_attn_Wo", [D, D], F32, kind="ExternalInput").ap()
    Wilr = nc.dram_tensor("L0_attn_W_ilr", [32, D], F32, kind="ExternalInput").ap()
    bilr = nc.dram_tensor("L0_attn_b_ilr", [32], F32, kind="ExternalInput").ap()
    W1 = nc.dram_tensor("L0_attn_W1", [32, 64, 64], F32, kind="ExternalInput").ap()
    yT = nc.dram_tensor("yT", [D, TPC], F32, kind="ExternalOutput").ap()
    C = declare_consts(nc, ["c_ones_col", "c_ones_row", "c_misc", "c_id128",
                            "c_id128f", "c_mask_bsu", "c_mask_dtu", "c_tokidx",
                            "c_neg_id64"])

    wq_sc = nc.dram_tensor("wq_sc", [D, D], BF16, kind="Internal").ap()
    wk_sc = nc.dram_tensor("wk_sc", [D, D], BF16, kind="Internal").ap()
    wv_sc = nc.dram_tensor("wv_sc", [D, D], BF16, kind="Internal").ap()
    wo_sc = nc.dram_tensor("wo_sc", [D, D], BF16, kind="Internal").ap()
    wi_sc = nc.dram_tensor("wi_sc", [32, D], BF16, kind="Internal").ap()
    xt_sc = nc.dram_tensor("xt_sc", [D, TPC], F32, kind="Internal").ap()

    es = ExitStack()
    sb = lambda n, sh, dt: es.enter_context(nc.sbuf_tensor(n, sh, dt))
    psb = lambda n, sh, dt: es.enter_context(nc.psum_tensor(n, sh, dt))

    xn_s = sb("xn_s", [128, 16 * 512], BF16)
    xcT = sb("xcT", [128, 16 * 512], BF16)
    xbT = sb("xbT", [128, 16 * 512], BF16)
    xaT = sb("xaT", [128, 16 * 512], BF16)
    zbT = sb("zbT", [128, 16 * 512], BF16)
    wbuf = [sb(f"wbuf{i}", [128, 16 * 512], BF16) for i in range(2)]
    xnat = [sb(f"xnat{i}", [128, 2048], F32) for i in range(2)]
    sqs = sb("sqs", [128, 2048], F32)
    xnb = sb("xnb", [128, 2048], BF16)
    xfT = sb("xfT", [128, 2048], F32)
    x_fin = [sb(f"x_fin{i}", [128, 512], F32) for i in range(2)]
    ffin = [sb(f"ffin{i}", [128, 512], F32) for i in range(2)]
    xsq = [sb(f"xsq{i}", [128, 512], BF16) for i in range(2)]
    stc = sb("stc", [128, 4], F32)
    W1s_f = sb("W1s_f", [128, 16 * 64], F32)
    W1s_b = sb("W1s_b", [128, 16 * 64], BF16)
    ilrT = sb("ilrT", [32, 512], F32)
    co_n = sb("co_n", [128, 4 * 32], F32)
    nt_bf = [sb(f"nt_bf{i}", [128, 128], BF16) for i in range(2)]
    ptm = [sb(f"ptm{i}", [128, 128], F32) for i in range(2)]
    lxc = [sb(f"lxc{i}", [128, 128], BF16) for i in range(2)]
    b0_bf = [sb(f"b0_bf{i}", [128, 64], BF16) for i in range(2)]
    z_bf = [sb(f"z_bf{i}", [128, 64], BF16) for i in range(2)]
    dz_bf = [sb(f"dz_bf{i}", [128, 64], BF16) for i in range(2)]
    xbn_bf = [sb(f"xbn_bf{i}", [128, 64], BF16) for i in range(2)]
    zb_f = [sb(f"zb_f{i}", [128, 64], F32) for i in range(2)]
    rows = sb("rows", [1, 2 * 512], F32)
    rowt = sb("rowt", [1, 512], F32)
    rows_bf = sb("rows_bf", [1, 2 * 512], BF16)
    rb = sb("rb", [128, 512], F32)
    mrb = sb("mrb", [128, 512], F32)
    on_c = sb("on_c", [128, 1], BF16)
    on_r = sb("on_r", [1, 128], BF16)
    id128 = sb("id128", [128, 128], BF16)
    id128f = sb("id128f", [128, 128], F32)
    mbsu = sb("mbsu", [128, 128], F32)
    mdtu = sb("mdtu", [128, 128], F32)
    tokc = sb("tokc", [128, 1], F32)
    nid64 = sb("nid64", [128, 64], BF16)
    bic = sb("bic", [32, 1], F32)
    misc_sb = sb("misc_sb", [128, 8], F32)

    pb = [psb(f"pb{i}", [128, 512], F32 if i != 5 else BF16) for i in range(8)]

    S = {k: Sem(es.enter_context(nc.semaphore("sem_" + k))) for k in
         ["ld", "lx", "lw0", "lw1", "lq0", "lq1", "st0", "st1", "lf0", "lf1",
          "sxt", "sty", "sc", "vd", "mm"]}
    N = {k: 0 for k in S}
    loadrec = {}

    def dma(q, key, out, in_):
        N[key] += 16
        return q.dma_start(out=out, in_=in_).then_inc(S[key].h, 16)

    mm, sc, vd = Sched(), Sched(), Sched()
    eps5 = misc_sb[0:1, 0:1]
    eps5c = misc_sb[:, 0:1]

    def inc(ins, key):
        return ins.then_inc(S[key].h, 1)

    APROJ = ["c", "b", "a"]
    ASRC = {"c": wq_sc, "b": wk_sc, "a": wv_sc}
    ADST = {}

    def reg_slab(s):
        for r in range(4):
            sc.add(("asq", s), r)
            sc.add(("arst", s), r)
        sc.add("asig", s)
        for k in range(16):
            sc.add(("awq", s), k)
        sc.add("awr", s)
        vd.add("aw1c", s)
        for r in range(4):
            vd.add(("ast", s), r)
            vd.add(("axnb", s), r)
            for g in range(4):
                vd.add(("axe", s, r), g)
        for pn in APROJ:
            for j in range(16):
                vd.add(("ape", s, pn), j)
        vd.add("ail", s)
        for ct in range(4):
            vd.add(("aco", s), ct)
            for h in range(32):
                vd.add(("av1", s, ct), h)
                for st in range(7):
                    vd.add(("ahl", s, ct, h), st)
                vd.add(("av2", s, ct), h)
                vd.add(("av3", s, ct), h)
                vd.add(("av4", s, ct), h)
                vd.add(("av5", s, ct), h)
        for k in range(16):
            vd.add(("awe", s), k)
        vd.add("awr1", s)
        vd.add("awbf", s)
        for k in range(16):
            vd.add(("afin", s), k)
        for r in range(4):
            for g in range(4):
                mm.add(("anT", s, r), g)
        for pn in APROJ:
            for j in range(16):
                mm.add(("apj", s, pn), j)
        mm.add("ailm", s)
        for ct in range(4):
            mm.add(("abg", s), ct)
            for h in range(32):
                mm.add(("am1", s, ct), h)
                for st in range(7):
                    mm.add(("ah", s, ct, h), st)
                mm.add(("am2", s, ct), h)
                mm.add(("am3", s, ct), h)
                mm.add(("am4", s, ct), h)
                mm.add(("am5", s, ct), h)
                mm.add(("am6", s, ct), h)
        for j in range(16):
            mm.add(("awo", s, j // 4), j % 4)
            mm.add(("awsq", s), j)
        mm.add("awbc", s)

    def hsl(buf, h, ct):
        return buf[64 * (h % 2):64 * (h % 2) + 64,
                   (h // 2) * 512 + ct * 128:(h // 2) * 512 + (ct + 1) * 128]

    def w1sl(buf, h):
        return buf[64 * (h % 2):64 * (h % 2) + 64, (h // 2) * 64:(h // 2) * 64 + 64]

    with es:
        NSPILL = 65
        with nc.Block() as blk:
            @blk.gpsimd
            def _(q):
                for i in range(NSPILL):
                    b = i % 2
                    if i >= 2:
                        q.wait_ge(S["st%d" % b].h, 16 * (i // 2))
                    if i < 64:
                        w = [Wq, Wk, Wv, Wo][i // 16]
                        r = i % 16
                        dma(q, "lq%d" % b, wbuf[b][:, 0:2048], w[r * 128:(r + 1) * 128, :])
                    else:
                        dma(q, "lq%d" % b, wbuf[b][0:32, 0:2048], Wilr)

            @blk.sync
            def _(q):
                for i in range(NSPILL):
                    b = i % 2
                    q.wait_ge(S["lq%d" % b].h, 16 * (i // 2 + 1))
                    if i < 64:
                        wsc = [wq_sc, wk_sc, wv_sc, wo_sc][i // 16]
                        r = i % 16
                        dma(q, "st%d" % b, wsc[r * 128:(r + 1) * 128, :], wbuf[b][:, 0:2048])
                    else:
                        dma(q, "st%d" % b, wi_sc, wbuf[b][0:32, 0:2048])
                for ap_, dst in [(C["c_ones_col"], on_c[:]), (C["c_ones_row"], on_r[:]),
                                 (C["c_misc"], misc_sb[:]), (C["c_id128"], id128[:]),
                                 (C["c_id128f"], id128f[:]), (C["c_mask_bsu"], mbsu[:]),
                                 (C["c_mask_dtu"], mdtu[:]), (C["c_tokidx"], tokc[:]),
                                 (C["c_neg_id64"], nid64[0:64, :])]:
                    dma(q, "ld", dst, ap_)
                dma(q, "ld", nid64[64:128, :], C["c_neg_id64"])
                dma(q, "ld", bic[0:32, 0:1], bilr.rearrange("(p o) -> p o", o=1))

        ld_pre = N["ld"]
        st_sp = (N["st0"], N["st1"])

        NSLAB = 8
        for bi in range(2):
            for sl in range(4):
                s = bi * 4 + sl
                reg_slab(s)
                lx_base = N["lx"]
                sxt_base = N["sxt"]
                tok0 = sl * 512

                with nc.Block() as blk:
                    @blk.gpsimd
                    def _(q, s=s, sl=sl, bi=bi, tok0=tok0, lx_base=lx_base):
                        if sl == 0:
                            if s == 0:
                                q.wait_ge(S["st0"].h, st_sp[0])
                                q.wait_ge(S["st1"].h, st_sp[1])
                            else:
                                q.wait_ge(S["mm"].h, mm[("am5", s - 1, 3), 31])
                            for h in range(32):
                                dma(q, "lx", w1sl(W1s_f, h), W1[h])
                        def emit_load(r):
                            gi = s * 4 + r
                            if gi >= 2:
                                p = gi - 2
                                wait(q, S["vd"], vd[("axe", p // 4, p % 4), 3])
                            dma(q, "lq%d" % (r % 2), xnat[r % 2][:],
                                xN[bi, tok0 + r * 128: tok0 + (r + 1) * 128, :])
                            loadrec[("axn", s, r)] = N["lq%d" % (r % 2)]

                        def emit_stores(r):
                            for g in range(4):
                                wait(q, S["vd"], vd[("axe", s, r), g])
                                for i4 in range(4):
                                    k = g * 4 + i4
                                    dma(q, "sxt", xt_sc[k * 128:(k + 1) * 128,
                                                        bi * 2048 + tok0 + r * 128: bi * 2048 + tok0 + (r + 1) * 128],
                                        xfT[:, k * 128:(k + 1) * 128])
                        emit_load(0)
                        emit_load(1)
                        emit_stores(0)
                        emit_load(2)
                        emit_stores(1)
                        emit_load(3)
                        emit_stores(2)
                        emit_stores(3)
                        for k in range(16):
                            wait(q, S["vd"], vd[("afin", s), k])
                            dma(q, "sty", yT[k * 128:(k + 1) * 128,
                                             bi * 2048 + tok0: bi * 2048 + tok0 + 512],
                                ffin[k % 2][:])

                    @blk.sync
                    def _(q, s=s, bi=bi, tok0=tok0):
                        if s == 0:
                            q.wait_ge(S["st0"].h, st_sp[0])
                            q.wait_ge(S["st1"].h, st_sp[1])
                        lin = 0
                        for pn in APROJ:
                            wsc = ASRC[pn]
                            for oc in range(4):
                                b = lin % 2
                                gl = s * 13 + lin
                                if gl >= 2:
                                    q.wait_ge(S["mm"].h, loadrec[("achunk", gl - 2)])
                                for k in range(16):
                                    q.dma_start_transpose(
                                        wbuf[b][:, k * 512:(k + 1) * 512],
                                        wsc[oc * 512:(oc + 1) * 512, k * 128:(k + 1) * 128]).then_inc(S["lw%d" % b].h, 16)
                                N["lw%d" % b] += 256
                                loadrec[("awchunk", s, pn, oc)] = (b, N["lw%d" % b])
                                loadrec[("achunk", s * 13 + lin)] = mm[("apj", s, pn), oc * 4 + 3]
                                lin += 1
                        b = lin % 2
                        gl = s * 13 + lin
                        if gl >= 2:
                            q.wait_ge(S["mm"].h, loadrec[("achunk", gl - 2)])
                        for k in range(16):
                            q.dma_start_transpose(
                                wbuf[b][:, k * 32:(k + 1) * 32],
                                wi_sc[0:32, k * 128:(k + 1) * 128]).then_inc(S["lw%d" % b].h, 16)
                        N["lw%d" % b] += 256
                        loadrec[("ailr", s)] = (b, N["lw%d" % b])
                        loadrec[("achunk", s * 13 + 12)] = mm["ailm", s]
                        for oc in range(4):
                            b = 0
                            gl2 = s * 4 + oc
                            if oc == 0:
                                q.wait_ge(S["mm"].h, mm["ailm", s])
                            if gl2 >= 1:
                                s2, oc2 = divmod(gl2 - 1, 4)
                                q.wait_ge(S["mm"].h, mm[("awo", s2, oc2), 3])
                            for k in range(16):
                                q.dma_start_transpose(
                                    wbuf[b][:, k * 512:(k + 1) * 512],
                                    wo_sc[oc * 512:(oc + 1) * 512, k * 128:(k + 1) * 128]).then_inc(S["lw%d" % b].h, 16)
                            N["lw%d" % b] += 256
                            loadrec[("awodone", s, oc)] = (b, N["lw%d" % b])
                        for k in range(16):
                            gi = s * 16 + k
                            if gi >= 2:
                                p = gi - 2
                                wait(q, S["vd"], vd[("afin", p // 16), p % 16])
                            q.wait_ge(S["sxt"].h, sxt_base + 1024)
                            dma(q, "lf%d" % (k % 2), x_fin[k % 2][:],
                                xt_sc[k * 128:(k + 1) * 128,
                                      bi * 2048 + tok0: bi * 2048 + tok0 + 512])

                    @blk.scalar
                    def _(q, s=s, sl=sl, lx_base=lx_base):
                        if s == 0:
                            q.wait_ge(S["ld"].h, ld_pre)
                        for r in range(4):
                            q.wait_ge(S["lq%d" % (r % 2)].h, loadrec[("axn", s, r)])
                            if r >= 1:
                                wait(q, S["vd"], vd[("ast", s), r - 1])
                            inc(q.activation(sqs[:], xnat[r % 2][:], AF.Square), "sc")
                            wait(q, S["vd"], vd[("ast", s), r])
                            q.activation(stc[:, 1:2], stc[:, 1:2], AF.Ln, bias=eps5c)
                            q.drain()
                            inc(q.activation(stc[:, 1:2], stc[:, 1:2], AF.Exp, scale=-0.5), "sc")
                        q.wait_ge(S["mm"].h, mm["ailm", s])
                        inc(q.activation(ilrT[:], pb[4][0:32, :], AF.Sigmoid, bias=bic[0:32, 0:1]), "sc")
                        for k in range(16):
                            wait(q, S["vd"], vd[("awe", s), k])
                            if k >= 2:
                                q.wait_ge(S["mm"].h, mm[("awsq", s), k - 2])
                            inc(q.activation(xsq[k % 2][:], xn_s[:, k * 512:(k + 1) * 512], AF.Square), "sc")
                        wait(q, S["vd"], vd["awr1", s])
                        q.activation(rows[:, 512:1024], rows[:, 512:1024], AF.Ln, bias=eps5)
                        q.drain()
                        inc(q.activation(rows[:, 512:1024], rows[:, 512:1024], AF.Exp, scale=-0.5), "sc")

                    @blk.vector
                    def _(q, s=s, sl=sl, bi=bi, lx_base=lx_base):
                        if sl == 0:
                            q.wait_ge(S["lx"].h, lx_base + 512)
                            q.drain()
                            inc(q.tensor_copy(W1s_b[:], W1s_f[:]), "vd")
                        else:
                            inc(q.engine_nop(), "vd")
                        for r in range(4):
                            wait(q, S["sc"], sc[("asq", s), r])
                            q.tensor_reduce(stc[:, 0:1], xnat[r % 2][:], AXX, ALU.add)
                            q.tensor_reduce(stc[:, 1:2], sqs[:], AXX, ALU.add)
                            q.drain()
                            q.tensor_scalar(stc[:, 0:1], stc[:, 0:1], 1.0 / 2048, None, ALU.mult)
                            q.tensor_scalar(stc[:, 1:2], stc[:, 1:2], 1.0 / 2048, None, ALU.mult)
                            q.drain()
                            q.tensor_tensor(stc[:, 2:3], stc[:, 0:1], stc[:, 0:1], ALU.mult)
                            q.drain()
                            inc(q.tensor_tensor(stc[:, 1:2], stc[:, 1:2], stc[:, 2:3], ALU.subtract), "vd")
                            wait(q, S["sc"], sc[("arst", s), r])
                            q.drain()
                            inc(q.tensor_scalar(xnb[:], xnat[r % 2][:], stc[:, 0:1], stc[:, 1:2],
                                                ALU.subtract, ALU.mult), "vd")
                            for g in range(4):
                                q.wait_ge(S["mm"].h, mm[("anT", s, r), g])
                                if g == 0 and s * 4 + r >= 1:
                                    q.wait_ge(S["sxt"].h, 256 * (s * 4 + r))
                                for i4 in range(4):
                                    k = g * 4 + i4
                                    q.tensor_copy(xn_s[:, k * 512 + r * 128: k * 512 + (r + 1) * 128],
                                                  pb[5][:, (k % 4) * 128:(k % 4 + 1) * 128])
                                for i4 in range(4):
                                    k = g * 4 + i4
                                    ins = q.tensor_copy(xfT[:, k * 128:(k + 1) * 128],
                                                        pb[6][:, (k % 4) * 128:(k % 4 + 1) * 128])
                                inc(ins, "vd")
                        for pn in APROJ:
                            dstb = {"c": xcT, "b": xbT, "a": xaT}[pn]
                            for j in range(16):
                                q.wait_ge(S["mm"].h, mm[("apj", s, pn), j])
                                bank = pb[4] if j % 2 == 0 else pb[6]
                                inc(q.tensor_copy(dstb[:, j * 512:(j + 1) * 512], bank[:]), "vd")
                        wait(q, S["sc"], sc["asig", s])
                        inc(q.engine_nop(), "vd")
                        for ct in range(4):
                            q.wait_ge(S["mm"].h, mm[("abg", s), ct])
                            q.drain()
                            inc(q.tensor_scalar(co_n[:, ct * 32:(ct + 1) * 32], pb[6][:, 0:32],
                                                tokc[:, 0:1], None, ALU.mult), "vd")
                            for h in range(32):
                                hp = h % 2
                                bank = pb[hp]
                                co_col = co_n[:, ct * 32 + h: ct * 32 + h + 1]
                                q.wait_ge(S["mm"].h, mm[("am1", s, ct), h])
                                q.tensor_copy(b0_bf[hp][:], bank[:, 0:64])
                                q.tensor_tensor(ptm[hp][:], bank[:, 192:320], mbsu[:], ALU.mult)
                                q.drain()
                                q.tensor_scalar(nt_bf[hp][:], ptm[hp][:], co_col, -1.0, ALU.mult, ALU.mult)
                                q.drain()
                                inc(q.tensor_copy(z_bf[hp][:], b0_bf[hp][:]), "vd")
                                for st in range(7):
                                    q.wait_ge(S["mm"].h, mm[("ah", s, ct, h), st])
                                    q.drain()
                                    inc(q.tensor_tensor(z_bf[hp][:], b0_bf[hp][:], bank[:, 0:64], ALU.add), "vd")
                                q.tensor_tensor(ptm[hp][:], bank[:, 64:192], mbsu[:], ALU.mult)
                                q.drain()
                                q.tensor_scalar(lxc[hp][:], ptm[hp][:], co_col, -1.0, ALU.mult, ALU.mult)
                                q.drain()
                                q.tensor_tensor(ptm[hp][:], bank[:, 64:192], mdtu[:], ALU.mult)
                                q.drain()
                                q.tensor_copy(nt_bf[hp][:], ptm[hp][:])
                                q.drain()
                                inc(q.tensor_scalar(dz_bf[hp][:], z_bf[hp][:], co_col, None, ALU.mult), "vd")
                                q.wait_ge(S["mm"].h, mm[("am3", s, ct), h])
                                q.tensor_scalar(zb_f[hp][:], bank[:, 384:448], co_col, -1.0, ALU.mult, ALU.mult)
                                q.drain()
                                q.tensor_tensor(zb_f[hp][:], zb_f[hp][:], bank[:, 320:384], ALU.add)
                                q.drain()
                                q.tensor_copy(b0_bf[hp][:], zb_f[hp][:])
                                q.wait_ge(S["mm"].h, mm[("am4", s, ct), h])
                                inc(q.tensor_copy(xbn_bf[hp][:], pb[5][:, 256 + hp * 64: 256 + hp * 64 + 64]), "vd")
                                q.wait_ge(S["mm"].h, mm[("am5", s, ct), h])
                                q.drain()
                                q.tensor_tensor(w1sl(W1s_f, h), w1sl(W1s_f, h), pb[2 + hp][64 * hp:64 * hp + 64, 0:64], ALU.subtract)
                                q.drain()
                                inc(q.tensor_copy(w1sl(W1s_b, h), w1sl(W1s_f, h)), "vd")
                                q.wait_ge(S["mm"].h, mm[("am6", s, ct), h])
                                inc(q.tensor_copy(hsl(zbT, h, ct), pb[5][64 * hp:64 * hp + 64, 384:512]), "vd")
                        for j in range(16):
                            q.wait_ge(S["mm"].h, mm[("awo", s, j // 4), j % 4])
                            bank = pb[4] if j % 2 == 0 else pb[6]
                            inc(q.tensor_copy(xn_s[:, j * 512:(j + 1) * 512], bank[:]), "vd")
                        q.wait_ge(S["mm"].h, mm[("awsq", s), 15])
                        q.tensor_scalar(rows[:, 0:512], pb[7][0:1, :], 1.0 / 2048, None, ALU.mult)
                        q.tensor_scalar(rows[:, 512:1024], pb[0][0:1, :], 1.0 / 2048, None, ALU.mult)
                        q.drain()
                        q.tensor_tensor(rowt[0:1, :], rows[:, 0:512], rows[:, 0:512], ALU.mult)
                        q.drain()
                        inc(q.tensor_tensor(rows[:, 512:1024], rows[:, 512:1024], rowt[0:1, :], ALU.subtract), "vd")
                        wait(q, S["sc"], sc["awr", s])
                        q.drain()
                        q.tensor_copy(rows_bf[:, 0:512], rows[:, 512:1024])
                        q.tensor_tensor(rowt[0:1, :], rows[:, 0:512], rows[:, 512:1024], ALU.mult)
                        q.drain()
                        inc(q.tensor_copy(rows_bf[:, 512:1024], rowt[0:1, :]), "vd")
                        q.wait_ge(S["mm"].h, mm["awbc", s])
                        q.tensor_copy(rb[:], pb[7][:])
                        q.tensor_copy(mrb[:], pb[0][:])
                        q.drain()
                        for k in range(16):
                            gi = s * 16 + k
                            q.wait_ge(S["lf%d" % (k % 2)].h, 16 * (gi // 2 + 1))
                            if gi >= 1:
                                q.wait_ge(S["sty"].h, 16 * gi)
                            ff = ffin[k % 2]
                            q.tensor_tensor(ff[:], xn_s[:, k * 512:(k + 1) * 512], rb[:], ALU.mult)
                            q.drain()
                            q.tensor_tensor(ff[:], ff[:], mrb[:], ALU.subtract)
                            q.drain()
                            ins = q.tensor_tensor(ff[:], ff[:], x_fin[k % 2][:], ALU.add)
                            inc(ins, "vd")

                    @blk.tensor
                    def _(q, s=s, sl=sl):
                        if s >= 1:
                            wait(q, S["vd"], vd[("afin", s - 1), 15])
                        else:
                            q.wait_ge(S["ld"].h, ld_pre)
                        for r in range(4):
                            wait(q, S["vd"], vd[("axnb", s), r])
                            for g in range(4):
                                if g >= 1:
                                    wait(q, S["vd"], vd[("axe", s, r), g - 1])
                                elif r >= 1:
                                    wait(q, S["vd"], vd[("axe", s, r - 1), 3])
                                for i4 in range(4):
                                    k = g * 4 + i4
                                    q.transpose(pb[5][:, (k % 4) * 128:(k % 4 + 1) * 128],
                                                xnb[:, k * 128:(k + 1) * 128], id128[:])
                                for i4 in range(4):
                                    k = g * 4 + i4
                                    ins = q.transpose(pb[6][:, (k % 4) * 128:(k % 4 + 1) * 128],
                                                      xnat[r % 2][:, k * 128:(k + 1) * 128], id128f[:])
                                inc(ins, "mm")
                        wait(q, S["vd"], vd[("axe", s, 3), 3])
                        prev_pn = None
                        for pn in APROJ:
                            for oc in range(4):
                                b, lwv = loadrec[("awchunk", s, pn, oc)]
                                q.wait_ge(S["lw%d" % b].h, lwv)
                                for otl in range(4):
                                    j = oc * 4 + otl
                                    if j >= 2:
                                        wait(q, S["vd"], vd[("ape", s, pn), j - 2])
                                    elif prev_pn is not None:
                                        wait(q, S["vd"], vd[("ape", s, prev_pn), 14 + j])
                                    bank = pb[4] if j % 2 == 0 else pb[6]
                                    for k in range(16):
                                        ins = q.matmul(bank[:],
                                                       wbuf[b][:, k * 512 + otl * 128: k * 512 + (otl + 1) * 128],
                                                       xn_s[:, k * 512:(k + 1) * 512],
                                                       start=(k == 0), stop=(k == 15))
                                    inc(ins, "mm")
                            prev_pn = pn
                        b, lwv = loadrec[("ailr", s)]
                        q.wait_ge(S["lw%d" % b].h, lwv)
                        wait(q, S["vd"], vd[("ape", s, "a"), 15])
                        for k in range(16):
                            ins = q.matmul(pb[4][0:32, :], wbuf[b][:, k * 32:(k + 1) * 32],
                                           xn_s[:, k * 512:(k + 1) * 512],
                                           start=(k == 0), stop=(k == 15))
                        inc(ins, "mm")
                        for ct in range(4):
                            if ct >= 1:
                                wait(q, S["vd"], vd[("aco", s), ct - 1])
                            wait(q, S["sc"], sc["asig", s])
                            inc(q.transpose(pb[6][:, 0:32], ilrT[:, ct * 128:(ct + 1) * 128],
                                            id128f[0:32, 0:32]), "mm")
                            for h in range(32):
                                hp = h % 2
                                bank = pb[hp]
                                if h >= 2:
                                    wait(q, S["vd"], vd[("av5", s, ct), h - 2])
                                elif ct >= 1:
                                    wait(q, S["vd"], vd[("av5", s, ct - 1), h + 30])
                                else:
                                    wait(q, S["vd"], vd["aw1c", s])
                                wait(q, S["vd"], vd[("aco", s), ct])
                                q.matmul(bank[:, 0:64], hsl(xbT, h, ct), w1sl(W1s_b, h), start=True, stop=False)
                                q.matmul(bank[:, 0:64], hsl(xaT, h, ct), nid64[64 * hp:64 * hp + 64, :], start=False, stop=True)
                                q.matmul(bank[:, 64:192], hsl(xbT, h, ct), hsl(xcT, h, ct), start=True, stop=True)
                                inc(q.matmul(bank[:, 192:320], hsl(xbT, h, ct), hsl(xbT, h, ct), start=True, stop=True), "mm")
                                for st in range(7):
                                    if st == 0:
                                        wait(q, S["vd"], vd[("av1", s, ct), h])
                                    else:
                                        wait(q, S["vd"], vd[("ahl", s, ct, h), st - 1])
                                    inc(q.matmul(bank[:, 0:64], nt_bf[hp][:], z_bf[hp][:], start=True, stop=True), "mm")
                                wait(q, S["vd"], vd[("av2", s, ct), h])
                                q.matmul(bank[:, 320:384], hsl(xcT, h, ct), w1sl(W1s_b, h), start=True, stop=False)
                                inc(q.matmul(bank[:, 320:384], lxc[hp][:], z_bf[hp][:], start=False, stop=True), "mm")
                                inc(q.matmul(bank[:, 384:448], nt_bf[hp][:], z_bf[hp][:], start=True, stop=True), "mm")
                                if h >= 1:
                                    wait(q, S["vd"], vd[("av5", s, ct), h - 1])
                                elif ct >= 1:
                                    wait(q, S["vd"], vd[("av5", s, ct - 1), 31])
                                inc(q.transpose(pb[5][:, 256 + hp * 64: 256 + hp * 64 + 64],
                                                hsl(xbT, h, ct),
                                                id128[64 * hp:64 * hp + 64, 64 * hp:64 * hp + 64]), "mm")
                                wait(q, S["vd"], vd[("av3", s, ct), h])
                                if h >= 2:
                                    wait(q, S["vd"], vd[("av4", s, ct), h - 2])
                                elif ct >= 1:
                                    wait(q, S["vd"], vd[("av4", s, ct - 1), h + 30])
                                inc(q.matmul(pb[2 + hp][64 * hp:64 * hp + 64, 0:64], xbn_bf[hp][:], dz_bf[hp][:],
                                             start=True, stop=True), "mm")
                                if h >= 2:
                                    wait(q, S["vd"], vd[("av5", s, ct), h - 2])
                                inc(q.transpose(pb[5][64 * hp:64 * hp + 64, 384:512], b0_bf[hp][:], id128[:]), "mm")
                        for j in range(16):
                            oc, otl = j // 4, j % 4
                            b, lwv = loadrec[("awodone", s, oc)]
                            q.wait_ge(S["lw%d" % b].h, lwv)
                            if j == 0:
                                wait(q, S["vd"], vd[("av5", s, 3), 31])
                            if j >= 2:
                                wait(q, S["vd"], vd[("awe", s), j - 2])
                            bank = pb[4] if j % 2 == 0 else pb[6]
                            for k in range(16):
                                ins = q.matmul(bank[:],
                                               wbuf[b][:, k * 512 + otl * 128: k * 512 + (otl + 1) * 128],
                                               zbT[:, k * 512:(k + 1) * 512],
                                               start=(k == 0), stop=(k == 15))
                            inc(ins, "mm")
                            wait(q, S["vd"], vd[("awe", s), j])
                            q.matmul(pb[7][0:1, :], on_c[:], xn_s[:, j * 512:(j + 1) * 512],
                                     start=(j == 0), stop=(j == 15))
                            wait(q, S["sc"], sc[("awq", s), j])
                            inc(q.matmul(pb[0][0:1, :], on_c[:], xsq[j % 2][:],
                                         start=(j == 0), stop=(j == 15)), "mm")
                        wait(q, S["vd"], vd["awbf", s])
                        q.matmul(pb[7][:], on_r[:], rows_bf[0:1, 0:512], start=True, stop=True)
                        inc(q.matmul(pb[0][:], on_r[:], rows_bf[0:1, 512:1024], start=True, stop=True), "mm")

        with nc.Block() as blk:
            @blk.gpsimd
            def _(q):
                q.wait_ge(S["sty"].h, N["sty"])
    return nc

_NC_CACHE = {}


def _get_nc(name):
    if name not in _NC_CACHE:
        _NC_CACHE[name] = {"a": build_phase_a, "b": build_phase_b,
                           "c": build_phase_c, "d": build_phase_d}[name]()
    return _NC_CACHE[name]


def kernel(**inputs):
    consts = host_consts()
    cores = list(range(NCORES))

    def run(name, per_core, weight_keys, const_keys):
        nc = _get_nc(name)
        im = []
        for c in cores:
            d = {k: inputs[k] for k in weight_keys}
            for k in const_keys:
                d[k] = consts[k]
            d.update(per_core[c])
            im.append(d)
        res = run_bass_kernel_spmd(nc, im, core_ids=cores)
        return res.results

    x = np.ascontiguousarray(inputs["L0_attn_hidden_states"])

    pc = [{"x": np.ascontiguousarray(x[2 * c:2 * c + 2])} for c in cores]
    ra = run("a", pc,
             ["L0_attn_Wq", "L0_attn_Wk", "L0_attn_Wv", "L0_attn_Wo",
              "L0_attn_W_ilr", "L0_attn_b_ilr", "L0_attn_W1"],
             ["c_ones_col", "c_ones_row", "c_misc", "c_id128", "c_id128f",
              "c_mask_bsu", "c_mask_dtu", "c_tokidx", "c_neg_id64"])

    pc = [{"xT": ra[c]["yT"]} for c in cores]
    rbv = run("b", pc,
              ["L0_ffn_agg_w", "L0_ffn_agg_b", "L0_ffn_in_gamma", "L0_ffn_in_beta",
               "L0_ffn_pn_gamma", "L0_ffn_pn_beta", "L0_ffn_d1_w", "L0_ffn_d1_b",
               "L0_ffn_d2_w", "L0_ffn_d2_b", "L0_ffn_bc_w", "L0_ffn_bc_b",
               "L0_ffn_gate_w", "L0_ffn_gate_b", "L0_ffn_out_gamma", "L0_ffn_out_beta"],
              ["c_ones_col", "c_ones_row", "c_misc"])

    pc = [{"xT": rbv[c]["yT"]} for c in cores]
    rc = run("c", pc,
             ["L1_attn_Wq", "L1_attn_Wk", "L1_attn_Wv", "L1_attn_Wb", "L1_attn_Wa",
              "L1_attn_Wg", "L1_attn_Wo", "L1_attn_conv_wq", "L1_attn_conv_wk",
              "L1_attn_conv_wv", "L1_attn_A_log", "L1_attn_dt_bias", "L1_attn_norm_w"],
             ["c_ones_col", "c_ones_row", "c_ones_rowf", "c_misc", "c_id128",
              "c_id128f", "c_mask_ls", "c_mask_ui"])

    pc = [{"xT": rc[c]["yT"]} for c in cores]
    rd = run("d", pc,
             ["L1_ffn_gate_w", "L1_ffn_gate_scale", "L1_ffn_up_w", "L1_ffn_up_scale",
              "L1_ffn_down_w", "L1_ffn_down_scale"],
             ["c_ones_col", "c_ones_row", "c_id128f", "c_misc"])

    out = np.empty((16, SEQ, D), dtype=np.float32)
    for c in cores:
        out[2 * c:2 * c + 2] = rd[c]["y"]
    return out
